# version 15
# baseline (speedup 1.0000x reference)
import numpy as np
import ml_dtypes
import concourse.bass as bass
import concourse.mybir as mybir
from concourse.bass_utils import run_bass_kernel_spmd
from contextlib import ExitStack

F32 = mybir.dt.float32
BF16 = mybir.dt.bfloat16
AF = mybir.ActivationFunctionType
OP = mybir.AluOpType
AX = mybir.AxisListType.X


def _lnexp_set_id():
    from concourse.hw_specs import get_activation_tables
    _nc = bass.Bass("TRN2", target_bir_lowering=False)
    for i, fs in enumerate(get_activation_tables(_nc.m.arch).values()):
        if AF.Ln in fs and AF.Exp in fs:
            return i
    raise ValueError("no table with Ln+Exp")


SET_LNEXP = _lnexp_set_id()

P = 128
NT = 32
D = 1024
DH = 512
NS = 64
G3 = 3072
DQ = 256
EPS = 1e-5
NCORES = 8

_STAGE = 3
_DBG3 = False


def _span(ap):
    lo = ap.offset
    hi = lo + 1
    for step, count in ap.ap[1:]:
        hi += abs(step) * (count - 1)
    return (ap.tensor.name, lo, hi)


class Guard:

    def __init__(self, eng, sem_h):
        self.e = eng
        self.sem = sem_h
        self.count = 0
        self.recs = []
        self.waited = 0

    @staticmethod
    def _ov(a, b):
        return a[0] == b[0] and a[1] < b[2] and b[1] < a[2]

    def _sync(self, spans):
        m = 0
        for vv, ws in self.recs:
            if vv > m and any(self._ov(w, k) for w in ws for k in spans):
                m = vv
        if m > self.waited:
            self.e.wait_ge(self.sem, m)
            self.waited = m
            self.recs = [r for r in self.recs if r[0] > m]

    def emit(self, outs, ins, fn):
        self._sync([_span(a) for a in ins] + [_span(o) for o in outs])
        r = fn()
        r.then_inc(self.sem, 1)
        self.count += 1
        self.recs.append((self.count, [_span(o) for o in outs]))
        return self.count


def _build(stage, prev=None):
    V = (prev or {}).get
    _DBG3 = globals()['_DBG3']
    nc = bass.Bass("TRN2", target_bir_lowering=False)

    x_d = nc.dram_tensor("x", [NT * P, D], F32, kind="ExternalInput").ap()
    out_d = nc.dram_tensor("out", [NT * P, D], F32, kind="ExternalOutput").ap()
    w_d = {}
    for name, shp in [
        ("slot_init", [NS, D]), ("w1", [2 * D, DH]), ("b1", [DH]), ("w2", [DH, NS]),
        ("b2", [NS]), ("tau", [1]), ("Wih", [G3, D]), ("Whh", [G3, D]), ("bih", [G3]),
        ("bhh", [G3]), ("wq", [D, DQ]), ("bq", [DQ]), ("wk", [D, DQ]), ("bk", [DQ]),
        ("wv", [D, D]), ("bv", [D]), ("ln_g", [D]), ("ln_b", [D]), ("wvp", [D, D]),
        ("bvp", [D]), ("wo", [D, D]), ("bo", [D]),
    ]:
        w_d[name] = nc.dram_tensor(name, shp, F32, kind="ExternalInput").ap()
    ctri_d = nc.dram_tensor("c_tri", [P, P], BF16, kind="ExternalInput").ap()
    ceye_d = nc.dram_tensor("c_eye", [P, P], BF16, kind="ExternalInput").ap()
    ce2_d = nc.dram_tensor("c_e2", [P, NT], F32, kind="ExternalInput").ap()
    cone_d = nc.dram_tensor("c_one", [P, P], BF16, kind="ExternalInput").ap()
    whh_bf = nc.dram_tensor("whh_bf", [8 * G3, P], BF16, kind="Internal").ap()
    wih_bf = nc.dram_tensor("wih_bf", [8 * G3, P], BF16, kind="Internal").ap()
    wq_bf = nc.dram_tensor("wq_bf", [D, DQ], BF16, kind="Internal").ap()
    wk_bf = nc.dram_tensor("wk_bf", [D, DQ], BF16, kind="Internal").ap()
    wv_bf = nc.dram_tensor("wv_bf", [D, D], BF16, kind="Internal").ap()
    wvp_bf = nc.dram_tensor("wvp_bf", [D, D], BF16, kind="Internal").ap()
    wo_bf = nc.dram_tensor("wo_bf", [D, D], BF16, kind="Internal").ap()
    if stage == 2:
        dbg_al = nc.dram_tensor("dbg_al", [NT * P, NS], F32, kind="ExternalOutput").ap()
        dbg_si = nc.dram_tensor("dbg_si", [NS, D + 2], F32, kind="ExternalOutput").ap()
    if stage >= 3 and _DBG3:
        dbg3 = nc.dram_tensor("dbg3", [P, 8 * D], F32, kind="ExternalOutput").ap()

    x_r = x_d.rearrange("(j p) c -> p j c", p=P)
    out_r = out_d.rearrange("(j p) c -> p j c", p=P)

    es = ExitStack()
    with es:
        sbt = lambda name, shp, dt=F32: es.enter_context(nc.sbuf_tensor(name, shp, dt))
        sem = lambda name: es.enter_context(nc.semaphore(name))

        sb_x = sbt("sb_x", [P, NT * D])
        c_tri = sbt("s_tri", [P, P], BF16)
        c_eye = sbt("s_eye", [P, P], BF16)
        c_e2 = sbt("s_e2", [P, NT])
        c_one = sbt("s_one", [P, P], BF16)
        stat = sbt("stat", [P, 16 * NT])
        SM_MX, SM_VX, SM_G1, SM_S1, SM_B1 = 0, 1, 2, 3, 4
        SM_MC, SM_VC, SM_G2, SM_S2, SM_B2 = 5, 6, 7, 8, 9
        SM_M3, SM_V3, SM_G3, SM_S3, SM_B3 = 10, 11, 12, 13, 14
        tmp4 = sbt("tmp4", [P, 64])
        bn6x = sbt("bn6x", [P, NT * 12])
        sb_smT = sbt("sb_smT", [P, 8], BF16)
        sb_cT = sbt("sb_cT", [P, 4])
        sb_b1T = sbt("sb_b1T", [P, 4])
        sb_St = sbt("sb_St", [P, 8 * NS], BF16)
        sb_w1a = sbt("sb_w1a", [P, 8 * DH], BF16)
        sb_w2 = sbt("sb_w2", [P, 4 * NS], BF16)
        sb_tau = sbt("sb_tau", [P, 1])
        sb_invt = sbt("sb_invt", [P, 1])
        sb_ninv = sbt("sb_ninv", [P, 1])
        b3bf = sbt("b3bf", [P, NT], BF16)
        sb_alpha = sbt("sb_alpha", [P, NT * NS], BF16)
        sb_alphaT = sbt("sb_alphaT", [NS, NT * P], BF16)
        sb_gh = sbt("sb_gh", [P, 24 * NS])
        sb_bz = sbt("sb_bz", [1, G3], BF16)
        sb_b2row = sbt("sb_b2row", [1, NS], BF16)
        sb_bq = sbt("sb_bq", [P, 2])
        sb_bk = sbt("sb_bk", [P, 2])
        sb_brow = sbt("sb_brow", [1, D], BF16)
        sb_lng = sbt("sb_lng", [P, 8])
        sb_lnbT = sbt("sb_lnbT", [P, 8], BF16)

        semX = [sem(f"semX{c}") for c in range(8)]
        semC = sem("semC")
        semGi = sem("semGi"); semGa = sem("semGa"); semGb = sem("semGb")
        semGh = sem("semGh"); semGw = sem("semGw"); semGq = sem("semGq")
        semGz = sem("semGz")
        semWT = [sem(f"semWT{c}") for c in range(8)]
        sINI = sem("sINI")
        sCS = sem("sCS"); sBS = sem("sBS"); sX2 = sem("sX2")
        sTT = sem("sTT"); sHH = sem("sHH"); sWW = sem("sWW")
        sGHc = sem("sGHc"); sP3T = sem("sP3T")
        sOUT = sem("sOUT"); sDBG = sem("sDBG")
        sSVG = sem("sSVG"); sSSG = sem("sSSG")

        gv = Guard(nc.vector, sSVG)
        gs = Guard(nc.scalar, sSSG)
        val = {}
        v_ = nc.vector
        s_ = nc.scalar

        def VTT(out, a, b_, op):
            return gv.emit([out], [a, b_], lambda: v_.tensor_tensor(out, a, b_, op=op))

        def VTS(out, a, s1, s2, op0, op1=None):
            ins = [a] + [x for x in (s1, s2) if x is not None and not isinstance(x, float)]
            if op1 is None:
                return gv.emit([out], ins, lambda: v_.tensor_scalar(out, a, s1, None, op0=op0))
            return gv.emit([out], ins, lambda: v_.tensor_scalar(out, a, s1, s2, op0=op0, op1=op1))

        def VSTT(out, a, scl, b_, op0, op1, accum=None):
            ins = [a, b_] + ([scl] if not isinstance(scl, float) else [])
            outs = [out] + ([accum] if accum is not None else [])
            return gv.emit(outs, ins, lambda: v_.scalar_tensor_tensor(
                out, a, scl, b_, op0=op0, op1=op1, accum_out=accum))

        def VRED(out, a, op, absv=None):
            return gv.emit([out], [a], lambda: v_.tensor_reduce(
                out, a, axis=AX, op=op, apply_absolute_value=absv))

        def VBN(out, a):
            return gv.emit([out], [a], lambda: v_.bn_stats(out, a))

        def VCP(out, a):
            return gv.emit([out], [a], lambda: v_.tensor_copy(out, a))

        def VRC(out, a):
            return gv.emit([out], [a], lambda: v_.reciprocal(out, a))

        def VMX(out, a):
            return gv.emit([out], [a], lambda: v_.max(out=out, in_=a))

        def SACT(out, in_, func, bias=0.0, scale=1.0, accum=None):
            extra = [x for x in (bias, scale) if not isinstance(x, float)]
            outs = [out] + ([accum] if accum is not None else [])
            return gs.emit(outs, [in_] + extra,
                           lambda: s_.activation(out, in_, func, bias=bias, scale=scale,
                                                 accum_out=accum))

        def SC(k, j, n=1):
            return stat[:, k * NT + j:k * NT + j + n]

        def combine(km, kv, bn6, pj, n, extra_e2):
            view = bn6[:].rearrange("p (j g k) -> p j g k", k=3, g=4)[:, pj:pj + n]
            mus = view[:, :, :, 1]
            cvs = view[:, :, :, 2]
            t0 = tmp4[:, 0:4 * n].rearrange("p (j g) -> p j g", g=4)
            a2 = tmp4[:, 16:16 + n]
            b1_ = tmp4[:, 20:20 + n]
            b2_ = tmp4[:, 24:24 + n]
            b3_ = tmp4[:, 28:28 + n]
            c1_ = tmp4[:, 32:32 + n]
            VRED(SC(km, pj, n), mus, OP.add)
            VTT(t0, mus, mus, OP.mult)
            VRED(a2, cvs, OP.add)
            VRED(b1_, t0, OP.add)
            VTT(b2_, SC(km, pj, n), SC(km, pj, n), OP.mult)
            if extra_e2 is None:
                VTS(b3_, a2, 1.0 / 1024.0, EPS, OP.mult, OP.add)
            else:
                VSTT(b3_, a2, 1.0 / 1024.0, extra_e2, OP.mult, OP.add)
            VSTT(c1_, b1_, 0.25, b3_, OP.mult, OP.add)
            return VSTT(SC(kv, pj, n), b2_, -1.0 / 16.0, c1_, OP.mult, OP.add)

        def post(ks, kb, kg, km, pj, n, sco=40):
            VRC(SC(ks, pj, n), SC(kg, pj, n))
            VTS(tmp4[:, sco:sco + n], SC(km, pj, n), -0.25, None, OP.mult)
            return VTT(SC(kb, pj, n), tmp4[:, sco:sco + n], SC(ks, pj, n), OP.mult)

        NG = NT // 4
        NPAIR = NT // 2

        blk = es.enter_context(nc.Block())

        @blk.sync
        def _(s):
            s.dma_start(out=c_tri[:], in_=ctri_d).then_inc(semC, 16)
            s.dma_start(out=c_eye[:], in_=ceye_d).then_inc(semC, 16)
            s.dma_start(out=c_e2[:], in_=ce2_d).then_inc(semC, 16)
            s.dma_start(out=c_one[:], in_=cone_d).then_inc(semC, 16)
            for ch in range(8):
                s.dma_start(out=sb_x[:].rearrange("p (j c) -> p j c", c=D)[:, 4 * ch:4 * ch + 4, :],
                            in_=x_r[:, 4 * ch:4 * ch + 4, :]).then_inc(semX[ch], 16)

        @blk.gpsimd
        def _(g):
            g.dma_start(out=sb_b1T[:], in_=w_d["b1"].rearrange("(c p) -> p c", p=P),
                        allow_slow_non_contiguous=True).then_inc(semGi, 16)
            g.dma_start(out=sb_tau[:], in_=w_d["tau"].rearrange("(o j) -> o j", o=1).broadcast_to((P, 1))).then_inc(semGi, 16)
            g.dma_start(out=sb_w1a[:].rearrange("p (c j) -> p c j", j=DH),
                        in_=w_d["w1"][0:D, :].rearrange("(c p) j -> p c j", p=P)).then_inc(semGa, 16)
            g.dma_start(out=sb_w2[:].rearrange("p (c j) -> p c j", j=NS),
                        in_=w_d["w2"].rearrange("(c p) j -> p c j", p=P)).then_inc(semGb, 16)
            g.dma_start(out=sb_b2row[:], in_=w_d["b2"].rearrange("(o j) -> o j", o=1)).then_inc(semGb, 16)
            g.dma_start(out=sb_bz[:], in_=w_d["bhh"].rearrange("(o j) -> o j", o=1)).then_inc(semGz, 16)

        es1 = ExitStack()
        sb1 = lambda name, shp, dt=F32: es1.enter_context(nc.sbuf_tensor(name, shp, dt))
        sb_Sbf = sb1("sb_Sbf", [NS, D], BF16)
        sb_w1b = sb1("sb_w1b", [P, 8 * DH], BF16)
        x1b = sb1("x1b", [P, 4 * D], BF16)
        anb = sb1("anb", [P, 3 * D])
        carry = sb1("carry", [1, D], BF16)
        bn6c = sb1("bn6c", [P, NT * 12])
        pcs = [es1.enter_context(nc.psum_tensor("pcs0", [P, D], F32)),
               es1.enter_context(nc.psum_tensor("pcs1", [P, D], F32))]
        ptot = es1.enter_context(nc.psum_tensor("ptot", [1, D], F32))
        es1i = ExitStack()
        p_init = es1i.enter_context(nc.psum_tensor("p_init", [P, DH], F32))
        p_ini_b = es1i.enter_context(nc.psum_tensor("p_ini_b", [P, 8 * NS], BF16))

        @blk.gpsimd
        def _(g):
            g.dma_start(out=sb_Sbf[:], in_=w_d["slot_init"]).then_inc(semGi, 16)
            g.dma_start(out=sb_w1b[:].rearrange("p (c j) -> p c j", j=DH),
                        in_=w_d["w1"][D:2 * D, :].rearrange("(c p) j -> p c j", p=P)).then_inc(semGi, 16)
            if stage >= 2:
                for ch in range(8):
                    g.wait_ge(semX[ch], 16)
                for cc in range(8):
                    g.dma_start(out=whh_bf[cc * G3:(cc + 1) * G3, :],
                                in_=w_d["Whh"][:, cc * P:(cc + 1) * P]).then_inc(semGh, 16)
                for cc in range(8):
                    g.dma_start(out=wih_bf[cc * G3:(cc + 1) * G3, :],
                                in_=w_d["Wih"][:, cc * P:(cc + 1) * P]).then_inc(semGw, 16)
                g.dma_start(out=wq_bf, in_=w_d["wq"]).then_inc(semGq, 16)
                g.dma_start(out=wk_bf, in_=w_d["wk"]).then_inc(semGq, 16)
                g.dma_start(out=wv_bf, in_=w_d["wv"]).then_inc(semGq, 16)
                g.dma_start(out=wvp_bf, in_=w_d["wvp"]).then_inc(semGq, 16)
                g.dma_start(out=wo_bf, in_=w_d["wo"]).then_inc(semGq, 16)

        @blk.vector
        def _(v):
            v.wait_ge(sINI, 1)
            VTS(sb_smT[:], p_init[:, 0:8], 1.0 / NS, None, OP.mult)
            VCP(sb_St[:], p_ini_b[:])
            v.wait_ge(semGi, 64)
            VRED(sb_invt[:], sb_tau[:], OP.max, absv=True)
            VTS(sb_invt[:], sb_invt[:], 0.1, None, OP.add)
            VRC(sb_invt[:], sb_invt[:])
            val["vini"] = VTS(sb_ninv[:], sb_invt[:], -1.0, None, OP.mult)
            v.wait_ge(sINI, 2)
            val["scT"] = VTT(sb_cT[:], p_init[:, 8:12], sb_b1T[:], OP.add)

        @blk.tensor
        def _(t):
            t.wait_ge(semGi, 64)
            t.wait_ge(semC, 64)
            for dc in range(8):
                t.matmul(p_init[:, dc:dc + 1], sb_Sbf[:, dc * P:(dc + 1) * P],
                         c_tri[0:NS, 127:128], start=True, stop=True)
            for dc in range(8):
                ins = t.transpose(p_ini_b[:, dc * NS:(dc + 1) * NS],
                                  sb_Sbf[:, dc * P:(dc + 1) * P], c_eye[0:NS, 0:NS])
            ins.then_inc(sINI, 1)
            t.wait_ge(sSVG, V("vini", 0))
            w1bv = sb_w1b[:].rearrange("p (c j) -> p c j", j=DH)
            for qc in range(4):
                for dc in range(8):
                    ins = t.matmul(p_init[:, 8 + qc:9 + qc],
                                   w1bv[:, dc, qc * P:(qc + 1) * P],
                                   sb_smT[:, dc:dc + 1], start=(dc == 0), stop=(dc == 7))
            ins.then_inc(sINI, 1)

        es1i.close()
        pcs.append(es1.enter_context(nc.psum_tensor("pcs2", [P, D], F32)))

        @blk.vector
        def _(v):
            v.wait_ge(semC, 64)
            for g in range(NG):
                v.wait_ge(semX[g], 16)
                for j in range(4 * g, 4 * g + 4):
                    VBN(bn6x[:, j * 12:j * 12 + 6], sb_x[:, j * D:j * D + DH])
                    VBN(bn6x[:, j * 12 + 6:j * 12 + 12], sb_x[:, j * D + DH:(j + 1) * D])
                val["cA", g] = combine(SM_MX, SM_VX, bn6x, 4 * g, 4, None)
                v.wait_ge(sSSG, V(("sq1", g), 0))
                val["pA", g] = post(SM_S1, SM_B1, SM_G1, SM_MX, 4 * g, 4)
                for pr in (2 * g, 2 * g + 1):
                    if pr >= 1:
                        v.wait_ge(sSSG, V(("sq2", pr - 1), 0))
                        val["pC", pr - 1] = post(SM_S2, SM_B2, SM_G2, SM_MC, 2 * (pr - 1), 2, sco=44)
                    v.wait_ge(sCS, 2 * pr + 2)
                    for j in (2 * pr, 2 * pr + 1):
                        VBN(bn6c[:, j * 12:j * 12 + 6], pcs[j % 3][:, 0:DH])
                        val["bn6c", pr] = VBN(bn6c[:, j * 12 + 6:j * 12 + 12], pcs[j % 3][:, DH:D])
                    val["cC", pr] = combine(SM_MC, SM_VC, bn6c, 2 * pr, 2, c_e2[:, 2 * pr:2 * pr + 2])
            v.wait_ge(sSSG, V(("sq2", NPAIR - 1), 0))
            val["pC", NPAIR - 1] = post(SM_S2, SM_B2, SM_G2, SM_MC, 2 * (NPAIR - 1), 2, sco=44)

        @blk.scalar
        def _(sc):
            sc.preload_act_table(AF.Sqrt)
            sc.wait_ge(sINI, 2)
            pass
            for j in range(NT + 3):
                if j >= 2 and j % 2 == 0 and (j - 2) // 2 < NPAIR:
                    pr = (j - 2) // 2
                    sc.wait_ge(sSVG, V(("cC", pr), 0))
                    val["sq2", pr] = SACT(SC(SM_G2, 2 * pr, 2), SC(SM_VC, 2 * pr, 2), AF.Sqrt)
                if j >= 1 and j % 2 == 0 and (j // 2 - 1) < NT // 2:
                    sb_i = j // 2 - 1
                    sc.wait_ge(sBS, sb_i + 1)
                    val["cv", sb_i] = SACT(carry[:], ptot[0:1, :], AF.Identity)
                if j % 4 == 0 and j < NT:
                    g = j // 4
                    sc.wait_ge(sSVG, V(("cA", g), 0))
                    val["sq1", g] = SACT(SC(SM_G1, 4 * g, 4), SC(SM_VX, 4 * g, 4), AF.Sqrt)
                if j < NT:
                    sc.wait_ge(sSVG, V(("pA", j // 4), 0))
                    if j >= 4:
                        sc.wait_ge(sCS, j - 3)
                        sc.wait_ge(sBS, (j - 2) // 2)
                    val["x1", j] = SACT(x1b[:, (j % 4) * D:(j % 4 + 1) * D],
                                        sb_x[:, j * D:(j + 1) * D], AF.Identity,
                                        bias=SC(SM_B1, j), scale=SC(SM_S1, j))
                if j >= 2 and (j - 2) < NT:
                    jj = j - 2
                    sc.wait_ge(sSVG, V(("pC", jj // 2), 0))
                    if jj >= 3:
                        sc.wait_ge(sX2, jj - 2)
                    val["an", jj] = SACT(anb[:, (jj % 3) * D:(jj % 3 + 1) * D],
                                         pcs[jj % 3][:], AF.Identity,
                                         bias=SC(SM_B2, jj), scale=SC(SM_S2, jj))

        @blk.tensor
        def _(t):
            t.wait_ge(sSVG, V("scT", 0))
            for j in range(NT):
                t.wait_ge(sSSG, V(("x1", j), 0))
                if j >= 2:
                    t.wait_ge(sSSG, V(("cv", j // 2 - 1), 0))
                if j >= 3:
                    t.wait_ge(sSVG, V(("bn6c", (j - 3) // 2), 0))
                    t.wait_ge(sSSG, V(("an", j - 3), 0))
                b = pcs[j % 3]
                x1t = x1b[:, (j % 4) * D:(j % 4 + 1) * D]
                x1p = x1b[:, ((j - 1) % 4) * D:((j - 1) % 4 + 1) * D]
                for h in range(2):
                    ins = t.matmul(b[:, h * DH:(h + 1) * DH], c_tri[:],
                                   x1t[:, h * DH:(h + 1) * DH], start=True, stop=(j == 0))
                if j % 2 == 1:
                    for h in range(2):
                        ins = t.matmul(b[:, h * DH:(h + 1) * DH], c_one[:],
                                       x1p[:, h * DH:(h + 1) * DH], start=False, stop=(j == 1))
                if j >= 2:
                    for h in range(2):
                        ins = t.matmul(b[:, h * DH:(h + 1) * DH], c_tri[0:1, :],
                                       carry[:, h * DH:(h + 1) * DH], start=False, stop=True)
                ins.then_inc(sCS, 1)
                for h in range(2):
                    ins = t.matmul(ptot[0:1, h * DH:(h + 1) * DH], c_tri[:, 127:128],
                                   x1t[:, h * DH:(h + 1) * DH],
                                   start=(j % 2 == 0), stop=(j % 2 == 1 and j < 2))
                if j % 2 == 1 and j >= 2:
                    for h in range(2):
                        ins = t.matmul(ptot[0:1, h * DH:(h + 1) * DH], c_tri[0:1, 0:1],
                                       carry[:, h * DH:(h + 1) * DH], start=False, stop=True)
                if j % 2 == 1:
                    ins.then_inc(sBS, 1)

        @blk.gpsimd
        def _(g):
            for j in range(NT):
                g.wait_ge(sSSG, V(("an", j), 0))
                g.tensor_tensor(sb_x[:, j * D:(j + 1) * D], anb[:, (j % 3) * D:(j % 3 + 1) * D],
                                sb_x[:, j * D:(j + 1) * D], op=OP.add).then_inc(sX2, 1)

        if stage == 1:
            @blk.sync
            def _(s):
                for gg in range(8):
                    s.wait_ge(sX2, 4 * gg + 4)
                    s.dma_start(out=out_r[:, 4 * gg:4 * gg + 4, :],
                                in_=sb_x[:].rearrange("p (j c) -> p j c", c=D)[:, 4 * gg:4 * gg + 4, :]
                                ).then_inc(sOUT, 16)
                s.wait_ge(sOUT, 8 * 16)
            return nc, val

        es1.close()

        es2 = ExitStack()
        sb2 = lambda name, shp, dt=F32: es2.enter_context(nc.sbuf_tensor(name, shp, dt))
        whhT = sb2("whhT", [P, G3], BF16)
        dn_f = sb2("dn_f", [NS, 2])
        es2b = ExitStack()
        sb2b = lambda name, shp, dt=F32: es2b.enter_context(nc.sbuf_tensor(name, shp, dt))
        x3b = sb2b("x3b", [P, 2 * D], BF16)
        x3t4 = sb2b("x3t4", [P, 4 * D], BF16)
        htb4 = sb2b("htb4", [P, 4 * DH], BF16)
        sb_L = sb2b("sb_L", [P, NT * NS])
        sb_E = sb2b("sb_E", [P, 2 * NS])
        m8b = sb2b("m8b", [P, 16])
        aub = sb2b("aub", [P, 2 * NS])
        zcol = sb2b("zcol", [P, 8])
        x2bf = sb2b("x2bf", [P, 2 * D], BF16)
        if stage == 2:
            dbg_sb = sb2b("dbg_sb", [NS, D + 2], BF16)
        es2p = ExitStack()
        p_xt = [es2p.enter_context(nc.psum_tensor("p_xt0", [P, D], BF16)),
                es2p.enter_context(nc.psum_tensor("p_xt1", [P, D], BF16))]
        p_h4 = [es2p.enter_context(nc.psum_tensor(f"p_h4{i}", [P, DH], F32))
                for i in range(4)]
        p_lg = [es2p.enter_context(nc.psum_tensor("p_lg0", [P, DH], F32)),
                es2p.enter_context(nc.psum_tensor("p_lg1", [P, DH], F32))]

        @blk.sync
        def _(s):
            s.wait_ge(sCS, NT)
            s.wait_ge(sX2, NT)
            s.wait_ge(sSVG, V(("pC", NPAIR - 1), 0))
            s.wait_ge(sSSG, V(("an", NT - 1), 0))
            s.wait_ge(semGh, 128)
            for dc in range(8):
                if dc >= 1:
                    s.wait_ge(sGHc, dc)
                s.dma_start_transpose(whhT[:],
                                      whh_bf[dc * G3:(dc + 1) * G3, :]).then_inc(semWT[dc], 16)

        @blk.vector
        def _(v):
            def xt4(j):
                if j == 0:
                    v.wait_ge(sX2, NT)
                v.wait_ge(sTT, j + 1)
                v.wait_ge(sHH, j // 4)
                val["xt4", j] = VCP(
                    x3t4[:].rearrange("p (c t k) -> p c t k", t=4, k=P)[:, :, j % 4, :],
                    p_xt[j % 2][:].rearrange("p (c k) -> p c k", k=P))

            for g in range(NG):
                v.wait_ge(sX2, 4 * g + 4)
                for j in range(4 * g, 4 * g + 4):
                    VBN(bn6x[:, j * 12:j * 12 + 6], sb_x[:, j * D:j * D + DH])
                    VBN(bn6x[:, j * 12 + 6:j * 12 + 12], sb_x[:, j * D + DH:(j + 1) * D])
                val["cA3", g] = combine(SM_M3, SM_V3, bn6x, 4 * g, 4, None)
                if g >= 1:
                    v.wait_ge(sSSG, V(("sq3", g - 1), 0))
                    val["pA3", g - 1] = post(SM_S3, SM_B3, SM_G3, SM_M3, 4 * (g - 1), 4)
                    for j in range(4 * (g - 1), 4 * g):
                        xt4(j)
            v.wait_ge(sSSG, V(("sq3", NG - 1), 0))
            val["pA3", NG - 1] = post(SM_S3, SM_B3, SM_G3, SM_M3, 4 * (NG - 1), 4)
            val["nmu3"] = VCP(b3bf[:], stat[:, SM_B3 * NT:SM_B3 * NT + NT])
            for j in range(4 * (NG - 1), NT):
                xt4(j)

        @blk.scalar
        def _(sc):
            for G in range(NG + 2):
                if G < NG:
                    sc.preload_act_table(AF.Sqrt)
                    sc.wait_ge(sSVG, V(("cA3", G), 0))
                    val["sq3", G] = SACT(SC(SM_G3, 4 * G, 4), SC(SM_V3, 4 * G, 4), AF.Sqrt)
                    sc.preload_act_table(AF.Gelu)
                    for t4 in range(4):
                        j = 4 * G + t4
                        sc.wait_ge(sSVG, V(("pA3", G), 0))
                        if j >= 2:
                            sc.wait_ge(sTT, j - 1)
                        val["x3", j] = SACT(x3b[:, (j % 2) * D:(j % 2 + 1) * D],
                                            sb_x[:, j * D:(j + 1) * D], AF.Identity,
                                            bias=SC(SM_B3, j), scale=SC(SM_S3, j))
                if 0 <= G - 1 < NG:
                    H = G - 1
                    sc.wait_ge(sHH, H + 1)
                    if H >= 1:
                        sc.wait_ge(sWW, 4 * (H - 1) + 4)
                    for jc in range(4):
                        ins_g = SACT(htb4[:, jc * DH:(jc + 1) * DH], p_h4[jc][:],
                                     AF.Gelu, bias=sb_cT[:, jc:jc + 1])
                    val["gel", H] = ins_g
                if G >= 3 and (G - 3) % 2 == 0 and (G - 3) // 2 < 4:
                    bb = (G - 3) // 2
                    sc.wait_ge(sWW, 8 * bb + 8)
                    val["lg", bb] = SACT(sb_L[:, bb * 8 * NS:(bb + 1) * 8 * NS],
                                         p_lg[bb % 2][:], AF.Identity)

        @blk.tensor
        def _(t):
            t.wait_ge(sSVG, V(("bn6c", NPAIR - 1), 0))
            t.wait_ge(sSSG, V(("an", NT - 1), 0))
            t.wait_ge(sSSG, V(("cv", NT // 2 - 1), 0))
            t.wait_ge(sX2, NT)
            t.wait_ge(semGa, 16)
            t.wait_ge(semGb, 32)
            t.wait_ge(sSVG, V("scT", 0))
            w1av = sb_w1a[:].rearrange("p (c j) -> p c j", j=DH)
            w2v = sb_w2[:].rearrange("p (c j) -> p c j", j=NS)
            for G in range(NG + 2):
                if 0 <= G - 1 < NG:
                    H = G - 1
                    t.wait_ge(sSVG, V(("xt4", 4 * H + 3), 0))
                    if H >= 1:
                        t.wait_ge(sSSG, V(("gel", H - 1), 0))
                    for dc in range(8):
                        for jc in range(4):
                            ins = t.matmul(p_h4[jc][:, 0:DH],
                                           w1av[:, dc, jc * P:(jc + 1) * P],
                                           x3t4[:, dc * DH:(dc + 1) * DH],
                                           start=(dc == 0), stop=(dc == 7))
                    ins.then_inc(sHH, 1)
                if 0 <= G - 2 < NG:
                    H2 = G - 2
                    t.wait_ge(sSSG, V(("gel", H2), 0))
                    if H2 >= 4 and H2 % 2 == 0:
                        t.wait_ge(sSSG, V(("lg", (H2 - 4) // 2), 0))
                    for t4 in range(4):
                        j = 4 * H2 + t4
                        plg = p_lg[(j // 8) % 2]
                        for jc in range(4):
                            t.matmul(plg[:, (j % 8) * NS:(j % 8 + 1) * NS],
                                     htb4[:, jc * DH + t4 * P:jc * DH + (t4 + 1) * P],
                                     w2v[:, jc, :],
                                     start=(j % 8 == 0 and jc == 0), stop=False)
                        ins = t.matmul(plg[:, (j % 8) * NS:(j % 8 + 1) * NS],
                                       c_one[0:1, :], sb_b2row[:],
                                       start=False, stop=(j % 8 == 7))
                        ins.then_inc(sWW, 1)
                if G < NG:
                    for t4 in range(4):
                        j = 4 * G + t4
                        t.wait_ge(sSSG, V(("x3", j), 0))
                        if j >= 2:
                            t.wait_ge(sSVG, V(("xt4", j - 2), 0))
                        x3v = x3b[:, (j % 2) * D:(j % 2 + 1) * D]
                        for dc in range(8):
                            ins = t.transpose(p_xt[j % 2][:, dc * P:(dc + 1) * P],
                                              x3v[:, dc * P:(dc + 1) * P], c_eye[:])
                        ins.then_inc(sTT, 1)

        es2p.close()
        es3p = ExitStack()
        p_si = es3p.enter_context(nc.psum_tensor("p_si", [NS, D], F32))
        p_d2 = es3p.enter_context(nc.psum_tensor("p_d2", [NS, 1], F32))
        p_at = es3p.enter_context(nc.psum_tensor("p_at", [NS, 4 * P], BF16))
        p_gh3 = es3p.enter_context(nc.psum_tensor("p_gh3", [P, 24 * NS], F32))

        @blk.vector
        def _(v):
            def altail(j):
                lt = sb_L[:, j * NS:(j + 1) * NS]
                v.wait_ge(sSSG, V(("exp", j), 0))
                ev_ = sb_E[:, (j % 2) * NS:(j % 2 + 1) * NS]
                au = aub[:, (j % 2) * NS:(j % 2 + 1) * NS]
                val["au", j] = VSTT(au, lt, m8b[:, (j % 2) * 8 + 7:(j % 2) * 8 + 8], ev_,
                                    OP.is_ge, OP.mult, accum=zcol[:, (j % 2) * 4:(j % 2) * 4 + 1])
                VRC(zcol[:, (j % 2) * 4 + 2:(j % 2) * 4 + 3], zcol[:, (j % 2) * 4:(j % 2) * 4 + 1])
                if j >= 2:
                    v.wait_ge(sP3T, j - 1)
                val["alf", j] = VTS(sb_alpha[:, j * NS:(j + 1) * NS], au,
                                    zcol[:, (j % 2) * 4 + 2:(j % 2) * 4 + 3], None, OP.mult)
                if j % 4 == 3:
                    v.wait_ge(sP3T, j + 1)
                    val["ate", j // 4] = VCP(sb_alphaT[:, (j - 3) * P:(j + 1) * P], p_at[0:NS, :])

            for j in range(NT):
                v.wait_ge(sSSG, V(("lg", j // 8), 0))
                if j >= 2:
                    v.wait_ge(sSSG, V(("exp", j - 2), 0))
                val["m8", j] = VMX(m8b[:, (j % 2) * 8:(j % 2) * 8 + 8], sb_L[:, j * NS:(j + 1) * NS])
                val["eb", j] = VTT(zcol[:, (j % 2) * 4 + 1:(j % 2) * 4 + 2],
                                   m8b[:, (j % 2) * 8:(j % 2) * 8 + 1], sb_ninv[:], OP.mult)
                if j >= 1:
                    altail(j - 1)
            altail(NT - 1)
            v.wait_ge(sGHc, 8)
            val["ghe"] = VCP(sb_gh[:], p_gh3[:])

        @blk.scalar
        def _(sc):
            sc.preload_act_table(AF.Exp)
            for j in range(NT):
                sc.wait_ge(sSVG, V(("eb", j), 0))
                if j >= 2:
                    sc.wait_ge(sSVG, V(("au", j - 2), 0))
                val["exp", j] = SACT(sb_E[:, (j % 2) * NS:(j % 2 + 1) * NS],
                                     sb_L[:, j * NS:(j + 1) * NS], AF.Exp,
                                     bias=zcol[:, (j % 2) * 4 + 1:(j % 2) * 4 + 2],
                                     scale=sb_invt[:])
                if j >= 2:
                    sc.wait_ge(sP3T, j - 1)
                val["x2b", j] = SACT(x2bf[:, (j % 2) * D:(j % 2 + 1) * D],
                                     sb_x[:, j * D:(j + 1) * D], AF.Identity,
                                     scale=SC(SM_S3, j))

        @blk.tensor
        def _(t):
            t.wait_ge(sSSG, V(("lg", 3), 0))
            t.wait_ge(sSVG, V(("xt4", NT - 1), 0))
            t.wait_ge(sSSG, V(("gel", NG - 1), 0))
            t.wait_ge(semGz, 16)
            for dc in range(8):
                t.wait_ge(semWT[dc], 16)
                for gc in range(24):
                    ins = t.matmul(p_gh3[:, gc * NS:(gc + 1) * NS],
                                   whhT[:, gc * P:(gc + 1) * P],
                                   sb_St[:, dc * NS:(dc + 1) * NS],
                                   start=(dc == 0 and gc % 8 == 0), stop=False)
                if dc == 7:
                    for gc in range(24):
                        ins = t.matmul(p_gh3[:, gc * NS:(gc + 1) * NS],
                                       sb_bz[:, gc * P:(gc + 1) * P],
                                       c_one[0:1, 0:NS], start=False, stop=(gc % 8 == 7))
                ins.then_inc(sGHc, 1)
            for j in range(NT):
                t.wait_ge(sSVG, V(("alf", j), 0))
                t.wait_ge(sSSG, V(("x2b", j), 0))
                alv = sb_alpha[:, j * NS:(j + 1) * NS]
                x2v = x2bf[:, (j % 2) * D:(j % 2 + 1) * D]
                for h in range(2):
                    t.matmul(p_si[0:NS, h * DH:(h + 1) * DH], alv, x2v[:, h * DH:(h + 1) * DH],
                             start=(j == 0), stop=(j == NT - 1))
                t.matmul(p_d2[0:NS, 0:1], alv, b3bf[:, j:j + 1],
                         start=(j == 0), stop=(j == NT - 1))
                if j % 4 == 0 and j >= 4:
                    t.wait_ge(sSVG, V(("ate", j // 4 - 1), 0))
                ins = t.transpose(p_at[0:NS, (j % 4) * P:(j % 4 + 1) * P], alv, c_eye[:])
                ins.then_inc(sP3T, 1)


        if stage >= 3:
            es2b.close()
            es4s = ExitStack()
            sb4 = lambda name, shp, dt=F32: es4s.enter_context(nc.sbuf_tensor(name, shp, dt))
            dn_f4 = sb4("dn_f4", [NS, 2])
            sb_d2c = sb4("sb_d2c", [NS, 1])
            wst = sb4("wst", [P, 4 * D], BF16)
            ycol = sb4("ycol", [NS, 8])
            lnbw = sb4("lnbw", [1, D], BF16)
            sb_sv = sb4("sb_sv", [NS, D], BF16)
            svT = sb4("svT", [P, 8 * NS], BF16)
            sb_sv2 = sb4("sb_sv2", [NS, D], BF16)
            es4s2 = ExitStack()
            sb42 = lambda name, shp, dt=F32: es4s2.enter_context(nc.sbuf_tensor(name, shp, dt))
            si_bf = sb42("si_bf", [NS, D], BF16)
            siT = sb42("siT", [P, 8 * NS], BF16)
            tn1 = sb42("tn1", [P, DH])
            tn2 = sb42("tn2", [P, DH])
            tn3 = sb42("tn3", [P, DH])
            SnT = sb42("SnT", [P, 8 * NS], BF16)
            qT = sb42("qT", [P, 2 * NS], BF16)
            kT = sb42("kT", [P, 2 * NS], BF16)
            sb_att = sb42("sb_att", [NS, NS])
            attbf = sb42("attbf", [NS, NS], BF16)
            attT = sb42("attT", [NS, NS], BF16)
            sb_v = sb42("sb_v", [NS, D], BF16)
            Snsl = sb42("Snsl", [NS, D], BF16)
            sb_yn = sb42("sb_yn", [NS, D], BF16)
            ynT = sb42("ynT", [P, 8 * NS], BF16)
            semL1 = sem("semL1")
            semW4 = [sem(f"semW4{c}") for c in range(8)]
            semW5 = [sem(f"semW5{c}") for c in range(8)]
            semWS = [sem(f"semWS{c}") for c in range(8)]
            semW2 = [sem(f"semW2{c}") for c in range(8)]
            semW3 = [sem(f"semW3{c}") for c in range(8)]
            sGIc = sem("sGIc")
            sWSc = sem("sWSc")
            sX5 = sem("sX5")
            sX5o = sem("sX5o")

            @blk.gpsimd
            def _(g):
                g.wait_ge(sGHc, 8)
                g.dma_start(out=sb_bz[:], in_=w_d["bih"].rearrange("(o j) -> o j", o=1)).then_inc(semGz, 16)
                g.dma_start(out=sb_bq[:].rearrange("p c -> p c"),
                            in_=w_d["bq"].rearrange("(c p) -> p c", p=P), allow_slow_non_contiguous=True).then_inc(semL1, 16)
                g.dma_start(out=sb_bk[:], in_=w_d["bk"].rearrange("(c p) -> p c", p=P), allow_slow_non_contiguous=True).then_inc(semL1, 16)
                g.dma_start(out=sb_brow[:], in_=w_d["bv"].rearrange("(o j) -> o j", o=1)).then_inc(semL1, 16)
                g.dma_start(out=sb_lng[:], in_=w_d["ln_g"].rearrange("(c p) -> p c", p=P), allow_slow_non_contiguous=True).then_inc(semL1, 16)
                g.dma_start(out=sb_lnbT[:], in_=w_d["ln_b"].rearrange("(c p) -> p c", p=P), allow_slow_non_contiguous=True).then_inc(semL1, 16)
                g.wait_ge(sGIc, 11)
                g.dma_start(out=sb_brow[:], in_=w_d["bvp"].rearrange("(o j) -> o j", o=1)).then_inc(semGz, 16)
                g.wait_ge(sGIc, 15)
                g.dma_start(out=sb_brow[:], in_=w_d["bo"].rearrange("(o j) -> o j", o=1)).then_inc(semGz, 16)

            @blk.sync
            def _(s):
                s.wait_ge(sP3T, NT)
                s.wait_ge(semGq, 16 * 5)
                s.wait_ge(sGHc, 8)
                s.wait_ge(semGw, 128)
                for dc in range(8):
                    if dc >= 1:
                        s.wait_ge(sGIc, dc)
                    s.dma_start_transpose(whhT[:], wih_bf[dc * G3:(dc + 1) * G3, :]).then_inc(semWT[dc], 16)
                for mi, (w_hbm, sems, wdt) in enumerate((
                        (wq_bf, semW4, DQ), (wk_bf, semW5, DQ), (wv_bf, semWS, D),
                        (wvp_bf, semW2, D), (wo_bf, semW3, D))):
                    for dc in range(8):
                        k = mi * 8 + dc
                        if k >= 4:
                            s.wait_ge(sWSc, k - 3)
                        s.dma_start(out=wst[:, (dc % 4) * D:(dc % 4) * D + wdt],
                                    in_=w_hbm.rearrange("(c p) q -> p c q", p=P)[:, dc, :]).then_inc(sems[dc], 16)

            @blk.vector
            def _(v):
                v.wait_ge(sP3T, NT)
                VRED(dn_f4[:, 0:1], sb_alphaT[:], OP.add)
                VTS(dn_f4[:, 1:2], dn_f4[:, 0:1], 1e-8, None, OP.add)
                VRC(dn_f4[:, 1:2], dn_f4[:, 1:2])
                VCP(sb_d2c[:], p_d2[0:NS, :])
                val["sie"] = VTS(si_bf[:], p_si[0:NS, :], sb_d2c[:], dn_f4[:, 1:2], OP.add, OP.mult)

            es3p.close()
            es4pA = ExitStack()
            p4a = es4pA.enter_context(nc.psum_tensor("p4a", [P, D], BF16))
            es4pB = ExitStack()
            p_qk = es4pB.enter_context(nc.psum_tensor("p_qk", [P, DH], F32))
            p_sc = es4pB.enter_context(nc.psum_tensor("p_sc", [NS, NS], F32))
            p_gi = es4pB.enter_context(nc.psum_tensor("p_gi", [P, 24 * NS], F32))

            @blk.tensor
            def _(t):
                t.wait_ge(sSVG, V("sie", 0))
                t.wait_ge(sSVG, V(("ate", NT // 4 - 1), 0))
                for dc in range(8):
                    ins = t.transpose(p4a[:, dc * NS:(dc + 1) * NS],
                                      si_bf[:, dc * P:(dc + 1) * P], c_eye[0:NS, 0:NS])
                ins.then_inc(sP3T, 1)

            @blk.vector
            def _(v):
                v.wait_ge(sP3T, NT + 1)
                val["siT"] = VCP(siT[:], p4a[:, 0:8 * NS])

            @blk.tensor
            def _(t):
                t.wait_ge(sSVG, V("siT", 0))
                t.wait_ge(semGz, 32)
                for dc in range(8):
                    t.wait_ge(semWT[dc], 32)
                    for gc in range(24):
                        ins = t.matmul(p_gi[:, gc * NS:(gc + 1) * NS],
                                       whhT[:, gc * P:(gc + 1) * P],
                                       siT[:, dc * NS:(dc + 1) * NS],
                                       start=(dc == 0 and gc % 8 == 0), stop=False)
                    if dc == 7:
                        for gc in range(24):
                            ins = t.matmul(p_gi[:, gc * NS:(gc + 1) * NS],
                                           sb_bz[:, gc * P:(gc + 1) * P],
                                           c_one[0:1, 0:NS], start=False, stop=(gc % 8 == 7))
                    ins.then_inc(sGIc, 1)

            @blk.vector
            def _(v):
                v.wait_ge(sGIc, 8)
                val["rp"] = VTT(tn1[:], p_gi[:, 0:DH], sb_gh[:, 0:DH], OP.add)
                val["zp"] = VTT(tn3[:], p_gi[:, DH:2 * DH], sb_gh[:, DH:2 * DH], OP.add)
                v.wait_ge(sSSG, V("rz", 0))
                VTT(tn2[:], tn1[:], sb_gh[:, 2 * DH:3 * DH], OP.mult)
                val["tn2"] = VTT(tn2[:], p_gi[:, 2 * DH:3 * DH], tn2[:], OP.add)
                v.wait_ge(sSSG, V("tanh", 0))
                VTT(tn1[:], sb_St[:], tn2[:], OP.subtract)
                VTT(tn1[:], tn3[:], tn1[:], OP.mult)
                val["snt"] = VTT(SnT[:], tn2[:], tn1[:], OP.add)

            @blk.scalar
            def _(sc):
                sc.preload_act_table(AF.Sigmoid)
                sc.wait_ge(sSVG, V("rp", 0))
                SACT(tn1[:], tn1[:], AF.Sigmoid)
                sc.wait_ge(sSVG, V("zp", 0))
                val["rz"] = SACT(tn3[:], tn3[:], AF.Sigmoid)
                sc.wait_ge(sSVG, V("tn2", 0))
                val["tanh"] = SACT(tn2[:], tn2[:], AF.Tanh)

            @blk.tensor
            def _(t):
                t.wait_ge(sSVG, V("snt", 0))
                t.wait_ge(semL1, 80)
                for mi, sems in ((0, semW4), (1, semW5)):
                    for dc in range(8):
                        t.wait_ge(sems[dc], 16)
                        wc = wst[:, (dc % 4) * D:(dc % 4) * D + DQ]
                        for qc in range(2):
                            ins = t.matmul(p_qk[:, mi * 2 * NS + qc * NS:mi * 2 * NS + (qc + 1) * NS],
                                           wc[:, qc * P:(qc + 1) * P],
                                           SnT[:, dc * NS:(dc + 1) * NS],
                                           start=(mi == 0 and dc == 0 and qc == 0),
                                           stop=(mi == 1 and dc == 7 and qc == 1))
                        ins.then_inc(sWSc, 1)

            @blk.scalar
            def _(sc):
                sc.wait_ge(sWSc, 16)
                for qc in range(2):
                    SACT(qT[:, qc * NS:(qc + 1) * NS], p_qk[:, qc * NS:(qc + 1) * NS],
                         AF.Identity, bias=sb_bq[:, qc:qc + 1])
                for qc in range(2):
                    ins_v = SACT(kT[:, qc * NS:(qc + 1) * NS], p_qk[:, 2 * NS + qc * NS:2 * NS + (qc + 1) * NS],
                                 AF.Identity, bias=sb_bk[:, qc:qc + 1])
                val["qke"] = ins_v

            @blk.tensor
            def _(t):
                t.wait_ge(sSSG, V("qke", 0))
                for qc in range(2):
                    ins = t.matmul(p_sc[0:NS, :], qT[:, qc * NS:(qc + 1) * NS],
                                   kT[:, qc * NS:(qc + 1) * NS],
                                   start=(qc == 0), stop=(qc == 1))
                ins.then_inc(sGIc, 1)

            @blk.vector
            def _(v):
                v.wait_ge(sGIc, 9)
                VRED(ycol[:, 0:1], p_sc[0:NS, :], OP.max)
                val["smx"] = VTS(ycol[:, 1:2], ycol[:, 0:1], -1.0 / 16.0, None, OP.mult)
                v.wait_ge(sSSG, V("attx", 0))
                VRC(ycol[:, 3:4], ycol[:, 2:3])
                val["attn"] = VTS(attbf[:], sb_att[:], ycol[:, 3:4], None, OP.mult)

            @blk.scalar
            def _(sc):
                sc.add_instruction(mybir.InstLoadActFuncSet(
                    name=nc.get_next_instruction_name(),
                    act_func_set_id=SET_LNEXP, ins=[], outs=[]))
                sc.wait_ge(sSVG, V("smx", 0))
                val["attx"] = SACT(sb_att[:], p_sc[0:NS, :], AF.Exp,
                                   bias=ycol[:, 1:2], scale=1.0 / 16.0, accum=ycol[:, 2:3])

            @blk.tensor
            def _(t):
                t.wait_ge(sSVG, V("attn", 0))
                t.transpose(p4a[0:NS, 0:NS], attbf[:], c_eye[0:NS, 0:NS]).then_inc(sGIc, 1)

            @blk.vector
            def _(v):
                v.wait_ge(sGIc, 10)
                val["attT"] = VCP(attT[:], p4a[0:NS, 0:NS])

            es4pB.close()
            es4pC = ExitStack()
            p_v = es4pC.enter_context(nc.psum_tensor("p_v", [NS, D], F32))
            p_scm = es4pC.enter_context(nc.psum_tensor("p_scm", [NS, D], F32))

            @blk.tensor
            def _(t):
                t.wait_ge(sSVG, V("attT", 0))
                for dc in range(8):
                    t.wait_ge(semWS[dc], 16)
                    wc = wst[:, (dc % 4) * D:(dc % 4 + 1) * D]
                    for h in range(2):
                        ins = t.matmul(p_v[0:NS, h * DH:(h + 1) * DH],
                                       SnT[:, dc * NS:(dc + 1) * NS], wc[:, h * DH:(h + 1) * DH],
                                       start=(dc == 0), stop=False)
                    ins.then_inc(sWSc, 1)
                t.wait_ge(semL1, 80)
                for h in range(2):
                    ins = t.matmul(p_v[0:NS, h * DH:(h + 1) * DH], c_one[0:1, 0:NS],
                                   sb_brow[:, h * DH:(h + 1) * DH], start=False, stop=True)
                ins.then_inc(sGIc, 1)

            @blk.scalar
            def _(sc):
                sc.wait_ge(sGIc, 11)
                val["ve"] = SACT(sb_v[:], p_v[0:NS, :], AF.Identity)

            @blk.tensor
            def _(t):
                t.wait_ge(sSSG, V("ve", 0))
                for dc in range(8):
                    ins2 = t.transpose(p4a[0:NS, dc * P:(dc + 1) * P],
                                       SnT[:, dc * NS:(dc + 1) * NS], c_eye[:])
                ins2.then_inc(sGIc, 1)

            @blk.vector
            def _(v):
                v.wait_ge(sGIc, 12)
                val["snsl"] = VCP(Snsl[:], p4a[0:NS, 0:D])

            @blk.tensor
            def _(t):
                t.wait_ge(sSVG, V("snsl", 0))
                for h in range(2):
                    t.matmul(p_scm[0:NS, h * DH:(h + 1) * DH], attT[:],
                             sb_v[:, h * DH:(h + 1) * DH], start=True, stop=False)
                for h in range(2):
                    ins = t.matmul(p_scm[0:NS, h * DH:(h + 1) * DH], c_eye[0:NS, 0:NS],
                                   Snsl[:, h * DH:(h + 1) * DH], start=False, stop=True)
                ins.then_inc(sGIc, 1)

            @blk.vector
            def _(v):
                v.wait_ge(sGIc, 13)
                VBN(bn6x[0:NS, 0:6], p_scm[0:NS, 0:DH])
                VBN(bn6x[0:NS, 6:12], p_scm[0:NS, DH:D])
                view = bn6x[0:NS, 0:12].rearrange("p (g k) -> p g k", k=3)
                mus = view[:, :, 1]
                cvs = view[:, :, 2]
                VRED(ycol[:, 4:5], mus, OP.add)
                VTT(tmp4[0:NS, 0:4], mus, mus, OP.mult)
                VRED(ycol[:, 5:6], cvs, OP.add)
                VRED(tmp4[0:NS, 4:5], tmp4[0:NS, 0:4], OP.add)
                VTT(tmp4[0:NS, 5:6], ycol[:, 4:5], ycol[:, 4:5], OP.mult)
                VTS(tmp4[0:NS, 6:7], ycol[:, 5:6], 1.0 / 1024.0, EPS, OP.mult, OP.add)
                VSTT(tmp4[0:NS, 7:8], tmp4[0:NS, 4:5], 0.25, tmp4[0:NS, 6:7], OP.mult, OP.add)
                val["yvar"] = VSTT(ycol[:, 6:7], tmp4[0:NS, 5:6], -1.0 / 16.0, tmp4[0:NS, 7:8], OP.mult, OP.add)
                v.wait_ge(sSSG, V("yrs", 0))
                VTS(tmp4[0:NS, 8:9], ycol[:, 4:5], -0.25, None, OP.mult)
                val["yb"] = VTT(ycol[:, 7:8], tmp4[0:NS, 8:9], ycol[:, 2:3], OP.mult)

            @blk.scalar
            def _(sc):
                sc.wait_ge(sSVG, V("yvar", 0))
                SACT(tmp4[0:NS, 9:10], ycol[:, 6:7], AF.Ln)
                val["yrs"] = SACT(ycol[:, 2:3], tmp4[0:NS, 9:10], AF.Exp, scale=-0.5)
                sc.wait_ge(sSVG, V("yb", 0))
                val["yn"] = SACT(sb_yn[:], p_scm[0:NS, :], AF.Identity,
                                 bias=ycol[:, 7:8], scale=ycol[:, 2:3])

            es4pC.close()
            es4pD = ExitStack()
            p_sv = es4pD.enter_context(nc.psum_tensor("p_sv", [NS, D], F32))
            p_lnb = es4pD.enter_context(nc.psum_tensor("p_lnb", [1, D], F32))
            p_sv2 = es4pD.enter_context(nc.psum_tensor("p_sv2", [NS, D], F32))

            @blk.tensor
            def _(t):
                t.wait_ge(sSSG, V("yn", 0))
                t.wait_ge(sSVG, V("snsl", 0))
                for dc in range(8):
                    ins = t.transpose(p4a[:, dc * NS:(dc + 1) * NS],
                                      sb_yn[:, dc * P:(dc + 1) * P], c_eye[0:NS, 0:NS])
                ins.then_inc(sGIc, 1)

            @blk.vector
            def _(v):
                v.wait_ge(sGIc, 14)
                v.wait_ge(semL1, 80)
                for dc in range(8):
                    ins_v = VTS(ynT[:, dc * NS:(dc + 1) * NS], p4a[:, dc * NS:(dc + 1) * NS],
                                sb_lng[:, dc:dc + 1], None, OP.mult)
                val["ynT"] = ins_v

            @blk.tensor
            def _(t):
                t.wait_ge(sSVG, V("ynT", 0))
                t.wait_ge(semL1, 80)
                for dc in range(8):
                    t.wait_ge(semW2[dc], 16)
                    wc = wst[:, (dc % 4) * D:(dc % 4 + 1) * D]
                    for h in range(2):
                        t.matmul(p_sv[0:NS, h * DH:(h + 1) * DH],
                                 ynT[:, dc * NS:(dc + 1) * NS], wc[:, h * DH:(h + 1) * DH],
                                 start=(dc == 0), stop=False)
                        ins = t.matmul(p_lnb[0:1, h * DH:(h + 1) * DH],
                                       sb_lnbT[:, dc:dc + 1], wc[:, h * DH:(h + 1) * DH],
                                       start=(dc == 0), stop=False)
                    ins.then_inc(sWSc, 1)
                t.wait_ge(semGz, 48)
                for h in range(2):
                    ins = t.matmul(p_lnb[0:1, h * DH:(h + 1) * DH], c_tri[0:1, 0:1],
                                   sb_brow[:, h * DH:(h + 1) * DH], start=False, stop=True)
                ins.then_inc(sGIc, 1)

            @blk.scalar
            def _(sc):
                sc.wait_ge(sGIc, 15)
                val["lnbw"] = SACT(lnbw[:], p_lnb[0:1, :], AF.Identity)

            @blk.tensor
            def _(t):
                t.wait_ge(sSSG, V("lnbw", 0))
                for h in range(2):
                    ins = t.matmul(p_sv[0:NS, h * DH:(h + 1) * DH], c_one[0:1, 0:NS],
                                   lnbw[:, h * DH:(h + 1) * DH], start=False, stop=True)
                ins.then_inc(sGIc, 1)

            @blk.scalar
            def _(sc):
                sc.wait_ge(sGIc, 16)
                val["sve"] = SACT(sb_sv[:], p_sv[0:NS, :], AF.Identity)

            @blk.tensor
            def _(t):
                t.wait_ge(sSSG, V("sve", 0))
                for dc in range(8):
                    ins = t.transpose(p4a[:, dc * NS:(dc + 1) * NS],
                                      sb_sv[:, dc * P:(dc + 1) * P], c_eye[0:NS, 0:NS])
                ins.then_inc(sGIc, 1)

            @blk.vector
            def _(v):
                v.wait_ge(sGIc, 17)
                val["svT"] = VCP(svT[:], p4a[:, 0:8 * NS])

            @blk.tensor
            def _(t):
                t.wait_ge(sSVG, V("svT", 0))
                for dc in range(8):
                    t.wait_ge(semW3[dc], 16)
                    wc = wst[:, (dc % 4) * D:(dc % 4 + 1) * D]
                    for h in range(2):
                        ins = t.matmul(p_sv2[0:NS, h * DH:(h + 1) * DH],
                                       svT[:, dc * NS:(dc + 1) * NS], wc[:, h * DH:(h + 1) * DH],
                                       start=(dc == 0), stop=False)
                    ins.then_inc(sWSc, 1)
                t.wait_ge(semGz, 64)
                for h in range(2):
                    ins = t.matmul(p_sv2[0:NS, h * DH:(h + 1) * DH], c_one[0:1, 0:NS],
                                   sb_brow[:, h * DH:(h + 1) * DH], start=False, stop=True)
                ins.then_inc(sGIc, 1)

            @blk.scalar
            def _(sc):
                sc.wait_ge(sGIc, 18)
                val["sv2e"] = SACT(sb_sv2[:], p_sv2[0:NS, :], AF.Identity)

            if _DBG3:
                @blk.gpsimd
                def _(g):
                    g.wait_ge(sGIc, 18)
                    g.wait_ge(sSSG, V("sv2e", 0))
                    g.dma_start(out=dbg3[0:NS, 0:D], in_=si_bf[:]).then_inc(sDBG, 16)
                    g.dma_start(out=dbg3[:, D:D + DH], in_=SnT[:]).then_inc(sDBG, 16)
                    g.dma_start(out=dbg3[0:NS, 2 * D:2 * D + NS], in_=attbf[:]).then_inc(sDBG, 16)
                    g.dma_start(out=dbg3[0:NS, 3 * D:4 * D], in_=sb_v[:]).then_inc(sDBG, 16)
                    g.dma_start(out=dbg3[0:NS, 4 * D:5 * D], in_=sb_yn[:]).then_inc(sDBG, 16)
                    g.dma_start(out=dbg3[0:NS, 5 * D:6 * D], in_=sb_sv[:]).then_inc(sDBG, 16)
                    g.dma_start(out=dbg3[0:NS, 6 * D:7 * D], in_=sb_sv2[:]).then_inc(sDBG, 16)
                    g.dma_start(out=dbg3[:, 7 * D:7 * D + 24 * NS // 2], in_=sb_gh[:, 0:768]).then_inc(sDBG, 16)
                    g.wait_ge(sDBG, 128)
            es4s2.close()
            fnb = sb4("fnb", [P, 4 * D])
            es4pD.close()
            es4pA.close()
            es5p = ExitStack()
            p_f = [es5p.enter_context(nc.psum_tensor(f"p_f{i}", [P, D], F32))
                   for i in range(4)]

            @blk.tensor
            def _(t):
                t.wait_ge(sSSG, V("sv2e", 0))
                t.wait_ge(sSVG, V(("ate", NT // 4 - 1), 0))
                for j in range(NT):
                    if j >= 4:
                        t.wait_ge(sSSG, V(("fn", j - 4), 0))
                        t.wait_ge(sSVG, V(("bn5", (j - 4) // 2), 0))
                    pf = p_f[j % 4]
                    for h in range(2):
                        ins = t.matmul(pf[:, h * DH:(h + 1) * DH],
                                       sb_alphaT[:, j * P:(j + 1) * P],
                                       sb_sv2[:, h * DH:(h + 1) * DH],
                                       start=True, stop=True)
                    ins.then_inc(sCS, 1)

            def _gp_count(m):
                return sum(1 for j2 in range(m + 1) if j2 % 4 != 1)

            @blk.vector
            def _(v):
                def p5tail(q):
                    v.wait_ge(sSSG, V(("rs5", q), 0))
                    VTS(tmp4[:, 48:48 + 2], SC(SM_MX, 2 * q, 2), -0.25, None, OP.mult)
                    val["p5", q] = VTT(SC(SM_B1, 2 * q, 2), tmp4[:, 48:48 + 2], SC(SM_S1, 2 * q, 2), OP.mult)

                for pr in range(NPAIR):
                    if pr >= 1:
                        p5tail(pr - 1)
                    v.wait_ge(sCS, NT + 2 * pr + 2)
                    for j in (2 * pr, 2 * pr + 1):
                        VBN(bn6x[:, j * 12:j * 12 + 6], p_f[j % 4][:, 0:DH])
                        val["bn5", pr] = VBN(bn6x[:, j * 12 + 6:j * 12 + 12], p_f[j % 4][:, DH:D])
                    val["c5", pr] = combine(SM_MX, SM_VX, bn6x, 2 * pr, 2, None)
                p5tail(NPAIR - 1)

            @blk.scalar
            def _(sc):
                if _DBG3:
                    sc.wait_ge(sDBG, 128)
                for j in range(NT + 2):
                    if j % 2 == 0 and j >= 2 and (j - 2) // 2 < NPAIR:
                        pr = (j - 2) // 2
                        sc.wait_ge(sSVG, V(("c5", pr), 0))
                        SACT(SC(SM_G1, 2 * pr, 2), SC(SM_VX, 2 * pr, 2), AF.Ln)
                        val["rs5", pr] = SACT(SC(SM_S1, 2 * pr, 2), SC(SM_G1, 2 * pr, 2), AF.Exp, scale=-0.5)
                    if j >= 2 and (j - 2) < NT:
                        jj = j - 2
                        sc.wait_ge(sSVG, V(("p5", jj // 2), 0))
                        if jj >= 4:
                            sc.wait_ge(sX5, jj - 3)
                        val["fn", jj] = SACT(fnb[:, (jj % 4) * D:(jj % 4 + 1) * D],
                                             p_f[jj % 4][:], AF.Identity,
                                             bias=SC(SM_B1, jj), scale=SC(SM_S1, jj))

            @blk.gpsimd
            def _(g):
                for j in range(NT):
                    g.wait_ge(sSSG, V(("fn", j), 0))
                    g.tensor_tensor(sb_x[:, j * D:(j + 1) * D], fnb[:, (j % 4) * D:(j % 4 + 1) * D],
                                    sb_x[:, j * D:(j + 1) * D], op=OP.add).then_inc(sX5, 1)

            @blk.sync
            def _(s):
                for gg in range(8):
                    s.wait_ge(sX5, 4 * gg + 4)
                    s.dma_start(out=out_r[:, 4 * gg:4 * gg + 4, :],
                                in_=sb_x[:].rearrange("p (j c) -> p j c", c=D)[:, 4 * gg:4 * gg + 4, :]
                                ).then_inc(sOUT, 16)
                s.wait_ge(sOUT, 8 * 16)

            es5p.close()
            es4s.close()

        if stage == 2:
            @blk.vector
            def _(v):
                v.wait_ge(sP3T, NT)
                VCP(dbg_sb[0:NS, 0:D], p_si[0:NS, :])
                VCP(dbg_sb[0:NS, D:D + 1], p_d2[0:NS, :])
                VRED(dn_f[:, 0:1], sb_alphaT[:], OP.add)
                val["dn"] = VCP(dbg_sb[0:NS, D + 1:D + 2], dn_f[:, 0:1])

            @blk.gpsimd
            def _(g):
                for gg in range(8):
                    g.wait_ge(sX2, 4 * gg + 4)
                g.dma_start(out=out_r[:, :, :],
                            in_=sb_x[:].rearrange("p (j c) -> p j c", c=D)).then_inc(sOUT, 16)
                g.wait_ge(sSVG, V(("alf", NT - 1), 0))
                g.dma_start(out=dbg_al.rearrange("(j p) c -> p j c", p=P),
                            in_=sb_alpha[:].rearrange("p (j c) -> p j c", c=NS)).then_inc(sDBG, 16)
                g.wait_ge(sSVG, V("dn", 0))
                g.dma_start(out=dbg_si, in_=dbg_sb[0:NS, :]).then_inc(sDBG, 16)
                g.wait_ge(sOUT, 16)
                g.wait_ge(sDBG, 32)

        if stage == 2:
            es3p.close()
            es2b.close()
        es2.close()

    return nc, val


def _consts():
    c_tri = np.triu(np.ones((P, P))).astype(ml_dtypes.bfloat16)
    c_eye = np.eye(P).astype(ml_dtypes.bfloat16)
    t = np.arange(NT * P, dtype=np.float64).reshape(NT, P).T
    c_e2 = (EPS * (t + 1.0) ** 2).astype(np.float32)
    c_one = np.ones((P, P)).astype(ml_dtypes.bfloat16)
    return c_tri, c_eye, c_e2, c_one


def kernel(**inputs):
    _, vals = _build(_STAGE)
    nc, _ = _build(_STAGE, vals)
    c_tri, c_eye, c_e2, c_one = _consts()
    x = np.ascontiguousarray(np.asarray(inputs["L0_attn_x"], dtype=np.float32))
    wdict = {}
    for k, v in inputs.items():
        if k == "L0_attn_x":
            continue
        wdict[k.replace("L0_ffn_", "")] = np.ascontiguousarray(np.asarray(v, dtype=np.float32))
    in_maps = []
    for c in range(NCORES):
        m = {"x": x[c], "c_tri": c_tri, "c_eye": c_eye, "c_e2": c_e2, "c_one": c_one}
        m.update(wdict)
        in_maps.append(m)
    res = run_bass_kernel_spmd(nc, in_maps, core_ids=list(range(NCORES)))
    return np.stack([r["out"] for r in res.results], axis=0)


# version 16 (final)
# speedup vs baseline: 1.1014x; 1.1014x over previous
import numpy as np
import ml_dtypes
import concourse.bass as bass
import concourse.mybir as mybir
from concourse.bass_utils import run_bass_kernel_spmd
from contextlib import ExitStack

F32 = mybir.dt.float32
BF16 = mybir.dt.bfloat16
AF = mybir.ActivationFunctionType
OP = mybir.AluOpType
AX = mybir.AxisListType.X


def _lnexp_set_id():
    from concourse.hw_specs import get_activation_tables
    _nc = bass.Bass("TRN2", target_bir_lowering=False)
    for i, fs in enumerate(get_activation_tables(_nc.m.arch).values()):
        if AF.Ln in fs and AF.Exp in fs:
            return i
    raise ValueError("no table with Ln+Exp")


SET_LNEXP = _lnexp_set_id()

P = 128
NT = 32
D = 1024
DH = 512
NS = 64
G3 = 3072
DQ = 256
EPS = 1e-5
NCORES = 8

_STAGE = 3
_DBG3 = False


def _span(ap):
    lo = ap.offset
    hi = lo + 1
    for step, count in ap.ap[1:]:
        hi += abs(step) * (count - 1)
    return (ap.tensor.name, lo, hi)


class Guard:

    def __init__(self, eng, sem_h):
        self.e = eng
        self.sem = sem_h
        self.count = 0
        self.recs = []
        self.waited = 0

    @staticmethod
    def _ov(a, b):
        return a[0] == b[0] and a[1] < b[2] and b[1] < a[2]

    def _sync(self, spans):
        m = 0
        for vv, ws in self.recs:
            if vv > m and any(self._ov(w, k) for w in ws for k in spans):
                m = vv
        if m > self.waited:
            self.e.wait_ge(self.sem, m)
            self.waited = m
            self.recs = [r for r in self.recs if r[0] > m]

    def emit(self, outs, ins, fn):
        self._sync([_span(a) for a in ins] + [_span(o) for o in outs])
        r = fn()
        r.then_inc(self.sem, 1)
        self.count += 1
        self.recs.append((self.count, [_span(o) for o in outs]))
        return self.count


def _build(stage, prev=None):
    V = (prev or {}).get
    _DBG3 = globals()['_DBG3']
    nc = bass.Bass("TRN2", target_bir_lowering=False)

    x_d = nc.dram_tensor("x", [NT * P, D], F32, kind="ExternalInput").ap()
    out_d = nc.dram_tensor("out", [NT * P, D], F32, kind="ExternalOutput").ap()
    w_d = {}
    for name, shp in [
        ("slot_init", [NS, D]), ("w1", [2 * D, DH]), ("b1", [DH]), ("w2", [DH, NS]),
        ("b2", [NS]), ("tau", [1]), ("Wih", [G3, D]), ("Whh", [G3, D]), ("bih", [G3]),
        ("bhh", [G3]), ("wq", [D, DQ]), ("bq", [DQ]), ("wk", [D, DQ]), ("bk", [DQ]),
        ("wv", [D, D]), ("bv", [D]), ("ln_g", [D]), ("ln_b", [D]), ("wvp", [D, D]),
        ("bvp", [D]), ("wo", [D, D]), ("bo", [D]),
    ]:
        w_d[name] = nc.dram_tensor(name, shp, F32, kind="ExternalInput").ap()
    ctri_d = nc.dram_tensor("c_tri", [P, P], BF16, kind="ExternalInput").ap()
    ceye_d = nc.dram_tensor("c_eye", [P, P], BF16, kind="ExternalInput").ap()
    ce2_d = nc.dram_tensor("c_e2", [P, NT], F32, kind="ExternalInput").ap()
    cone_d = nc.dram_tensor("c_one", [P, P], BF16, kind="ExternalInput").ap()
    whh_bf = nc.dram_tensor("whh_bf", [G3, D], BF16, kind="Internal").ap()
    wih_bf = nc.dram_tensor("wih_bf", [G3, D], BF16, kind="Internal").ap()
    wq_bf = nc.dram_tensor("wq_bf", [D, DQ], BF16, kind="Internal").ap()
    wk_bf = nc.dram_tensor("wk_bf", [D, DQ], BF16, kind="Internal").ap()
    wv_bf = nc.dram_tensor("wv_bf", [D, D], BF16, kind="Internal").ap()
    wvp_bf = nc.dram_tensor("wvp_bf", [D, D], BF16, kind="Internal").ap()
    wo_bf = nc.dram_tensor("wo_bf", [D, D], BF16, kind="Internal").ap()
    if stage == 2:
        dbg_al = nc.dram_tensor("dbg_al", [NT * P, NS], F32, kind="ExternalOutput").ap()
        dbg_si = nc.dram_tensor("dbg_si", [NS, D + 2], F32, kind="ExternalOutput").ap()
    if stage >= 3 and _DBG3:
        dbg3 = nc.dram_tensor("dbg3", [P, 8 * D], F32, kind="ExternalOutput").ap()

    x_r = x_d.rearrange("(j p) c -> p j c", p=P)
    out_r = out_d.rearrange("(j p) c -> p j c", p=P)

    es = ExitStack()
    with es:
        sbt = lambda name, shp, dt=F32: es.enter_context(nc.sbuf_tensor(name, shp, dt))
        sem = lambda name: es.enter_context(nc.semaphore(name))

        sb_x = sbt("sb_x", [P, NT * D])
        c_tri = sbt("s_tri", [P, P], BF16)
        c_eye = sbt("s_eye", [P, P], BF16)
        c_e2 = sbt("s_e2", [P, NT])
        c_one = sbt("s_one", [P, P], BF16)
        stat = sbt("stat", [P, 16 * NT])
        SM_MX, SM_VX, SM_G1, SM_S1, SM_B1 = 0, 1, 2, 3, 4
        SM_MC, SM_VC, SM_G2, SM_S2, SM_B2 = 5, 6, 7, 8, 9
        SM_M3, SM_V3, SM_G3, SM_S3, SM_B3 = 10, 11, 12, 13, 14
        tmp4 = sbt("tmp4", [P, 64])
        bn6x = sbt("bn6x", [P, NT * 12])
        sb_smT = sbt("sb_smT", [P, 8], BF16)
        sb_cT = sbt("sb_cT", [P, 4])
        sb_b1T = sbt("sb_b1T", [P, 4])
        sb_St = sbt("sb_St", [P, 8 * NS], BF16)
        sb_w1a = sbt("sb_w1a", [P, 8 * DH], BF16)
        sb_w2 = sbt("sb_w2", [P, 4 * NS], BF16)
        sb_tau = sbt("sb_tau", [P, 1])
        sb_invt = sbt("sb_invt", [P, 1])
        sb_ninv = sbt("sb_ninv", [P, 1])
        b3bf = sbt("b3bf", [P, NT], BF16)
        sb_alpha = sbt("sb_alpha", [P, NT * NS], BF16)
        sb_alphaT = sbt("sb_alphaT", [NS, NT * P], BF16)
        sb_gh = sbt("sb_gh", [P, 24 * NS])
        sb_bz = sbt("sb_bz", [1, G3], BF16)
        sb_b2row = sbt("sb_b2row", [1, NS], BF16)
        sb_bq = sbt("sb_bq", [P, 2])
        sb_bk = sbt("sb_bk", [P, 2])
        sb_brow = sbt("sb_brow", [1, D], BF16)
        sb_lng = sbt("sb_lng", [P, 8])
        sb_lnbT = sbt("sb_lnbT", [P, 8], BF16)

        semX = [sem(f"semX{c}") for c in range(8)]
        semC = sem("semC")
        semGi = sem("semGi"); semGa = sem("semGa"); semGb = sem("semGb")
        semGh = sem("semGh"); semGw = sem("semGw"); semGq = sem("semGq")
        semGz = sem("semGz")
        semWT = [sem(f"semWT{c}") for c in range(8)]
        sINI = sem("sINI")
        sCS = sem("sCS"); sBS = sem("sBS"); sX2 = sem("sX2")
        sTT = sem("sTT"); sHH = sem("sHH"); sWW = sem("sWW")
        sGHc = sem("sGHc"); sP3T = sem("sP3T")
        sOUT = sem("sOUT"); sDBG = sem("sDBG")
        sSVG = sem("sSVG"); sSSG = sem("sSSG")

        gv = Guard(nc.vector, sSVG)
        gs = Guard(nc.scalar, sSSG)
        val = {}
        v_ = nc.vector
        s_ = nc.scalar

        def VTT(out, a, b_, op):
            return gv.emit([out], [a, b_], lambda: v_.tensor_tensor(out, a, b_, op=op))

        def VTS(out, a, s1, s2, op0, op1=None):
            ins = [a] + [x for x in (s1, s2) if x is not None and not isinstance(x, float)]
            if op1 is None:
                return gv.emit([out], ins, lambda: v_.tensor_scalar(out, a, s1, None, op0=op0))
            return gv.emit([out], ins, lambda: v_.tensor_scalar(out, a, s1, s2, op0=op0, op1=op1))

        def VSTT(out, a, scl, b_, op0, op1, accum=None):
            ins = [a, b_] + ([scl] if not isinstance(scl, float) else [])
            outs = [out] + ([accum] if accum is not None else [])
            return gv.emit(outs, ins, lambda: v_.scalar_tensor_tensor(
                out, a, scl, b_, op0=op0, op1=op1, accum_out=accum))

        def VRED(out, a, op, absv=None):
            return gv.emit([out], [a], lambda: v_.tensor_reduce(
                out, a, axis=AX, op=op, apply_absolute_value=absv))

        def VBN(out, a):
            return gv.emit([out], [a], lambda: v_.bn_stats(out, a))

        def VCP(out, a):
            return gv.emit([out], [a], lambda: v_.tensor_copy(out, a))

        def VRC(out, a):
            return gv.emit([out], [a], lambda: v_.reciprocal(out, a))

        def VMX(out, a):
            return gv.emit([out], [a], lambda: v_.max(out=out, in_=a))

        def SACT(out, in_, func, bias=0.0, scale=1.0, accum=None):
            extra = [x for x in (bias, scale) if not isinstance(x, float)]
            outs = [out] + ([accum] if accum is not None else [])
            return gs.emit(outs, [in_] + extra,
                           lambda: s_.activation(out, in_, func, bias=bias, scale=scale,
                                                 accum_out=accum))

        def SC(k, j, n=1):
            return stat[:, k * NT + j:k * NT + j + n]

        def combine(km, kv, bn6, pj, n, extra_e2):
            view = bn6[:].rearrange("p (j g k) -> p j g k", k=3, g=4)[:, pj:pj + n]
            mus = view[:, :, :, 1]
            cvs = view[:, :, :, 2]
            t0 = tmp4[:, 0:4 * n].rearrange("p (j g) -> p j g", g=4)
            a2 = tmp4[:, 16:16 + n]
            b1_ = tmp4[:, 20:20 + n]
            b2_ = tmp4[:, 24:24 + n]
            b3_ = tmp4[:, 28:28 + n]
            c1_ = tmp4[:, 32:32 + n]
            VRED(SC(km, pj, n), mus, OP.add)
            VTT(t0, mus, mus, OP.mult)
            VRED(a2, cvs, OP.add)
            VRED(b1_, t0, OP.add)
            VTT(b2_, SC(km, pj, n), SC(km, pj, n), OP.mult)
            if extra_e2 is None:
                VTS(b3_, a2, 1.0 / 1024.0, EPS, OP.mult, OP.add)
            else:
                VSTT(b3_, a2, 1.0 / 1024.0, extra_e2, OP.mult, OP.add)
            VSTT(c1_, b1_, 0.25, b3_, OP.mult, OP.add)
            return VSTT(SC(kv, pj, n), b2_, -1.0 / 16.0, c1_, OP.mult, OP.add)

        def post(ks, kb, kg, km, pj, n, sco=40):
            VRC(SC(ks, pj, n), SC(kg, pj, n))
            VTS(tmp4[:, sco:sco + n], SC(km, pj, n), -0.25, None, OP.mult)
            return VTT(SC(kb, pj, n), tmp4[:, sco:sco + n], SC(ks, pj, n), OP.mult)

        NG = NT // 4
        NPAIR = NT // 2

        blk = es.enter_context(nc.Block())

        @blk.sync
        def _(s):
            s.dma_start(out=c_tri[:], in_=ctri_d).then_inc(semC, 16)
            s.dma_start(out=c_eye[:], in_=ceye_d).then_inc(semC, 16)
            s.dma_start(out=c_e2[:], in_=ce2_d).then_inc(semC, 16)
            s.dma_start(out=c_one[:], in_=cone_d).then_inc(semC, 16)
            for ch in range(8):
                s.dma_start(out=sb_x[:].rearrange("p (j c) -> p j c", c=D)[:, 4 * ch:4 * ch + 4, :],
                            in_=x_r[:, 4 * ch:4 * ch + 4, :]).then_inc(semX[ch], 16)

        @blk.gpsimd
        def _(g):
            g.dma_start(out=sb_b1T[:], in_=w_d["b1"].rearrange("(c p) -> p c", p=P),
                        allow_slow_non_contiguous=True).then_inc(semGi, 16)
            g.dma_start(out=sb_tau[:], in_=w_d["tau"].rearrange("(o j) -> o j", o=1).broadcast_to((P, 1))).then_inc(semGi, 16)
            g.dma_start(out=sb_w1a[:].rearrange("p (c j) -> p c j", j=DH),
                        in_=w_d["w1"][0:D, :].rearrange("(c p) j -> p c j", p=P)).then_inc(semGa, 16)
            g.dma_start(out=sb_w2[:].rearrange("p (c j) -> p c j", j=NS),
                        in_=w_d["w2"].rearrange("(c p) j -> p c j", p=P)).then_inc(semGb, 16)
            g.dma_start(out=sb_b2row[:], in_=w_d["b2"].rearrange("(o j) -> o j", o=1)).then_inc(semGb, 16)
            g.dma_start(out=sb_bz[:], in_=w_d["bhh"].rearrange("(o j) -> o j", o=1)).then_inc(semGz, 16)

        es1 = ExitStack()
        sb1 = lambda name, shp, dt=F32: es1.enter_context(nc.sbuf_tensor(name, shp, dt))
        sb_Sbf = sb1("sb_Sbf", [NS, D], BF16)
        sb_w1b = sb1("sb_w1b", [P, 8 * DH], BF16)
        x1b = sb1("x1b", [P, 4 * D], BF16)
        anb = sb1("anb", [P, 3 * D])
        carry = sb1("carry", [1, D], BF16)
        bn6c = sb1("bn6c", [P, NT * 12])
        pcs = [es1.enter_context(nc.psum_tensor("pcs0", [P, D], F32)),
               es1.enter_context(nc.psum_tensor("pcs1", [P, D], F32))]
        ptot = es1.enter_context(nc.psum_tensor("ptot", [1, D], F32))
        es1i = ExitStack()
        p_init = es1i.enter_context(nc.psum_tensor("p_init", [P, DH], F32))
        p_ini_b = es1i.enter_context(nc.psum_tensor("p_ini_b", [P, 8 * NS], BF16))

        @blk.gpsimd
        def _(g):
            g.dma_start(out=sb_Sbf[:], in_=w_d["slot_init"]).then_inc(semGi, 16)
            g.dma_start(out=sb_w1b[:].rearrange("p (c j) -> p c j", j=DH),
                        in_=w_d["w1"][D:2 * D, :].rearrange("(c p) j -> p c j", p=P)).then_inc(semGi, 16)
            if stage >= 2:
                for ch in range(8):
                    g.wait_ge(semX[ch], 16)
                g.dma_start(out=whh_bf, in_=w_d["Whh"]).then_inc(semGh, 16)
                g.dma_start(out=wih_bf, in_=w_d["Wih"]).then_inc(semGw, 16)
                g.dma_start(out=wq_bf, in_=w_d["wq"]).then_inc(semGq, 16)
                g.dma_start(out=wk_bf, in_=w_d["wk"]).then_inc(semGq, 16)
                g.dma_start(out=wv_bf, in_=w_d["wv"]).then_inc(semGq, 16)
                g.dma_start(out=wvp_bf, in_=w_d["wvp"]).then_inc(semGq, 16)
                g.dma_start(out=wo_bf, in_=w_d["wo"]).then_inc(semGq, 16)

        @blk.vector
        def _(v):
            v.wait_ge(sINI, 1)
            VTS(sb_smT[:], p_init[:, 0:8], 1.0 / NS, None, OP.mult)
            VCP(sb_St[:], p_ini_b[:])
            v.wait_ge(semGi, 64)
            VRED(sb_invt[:], sb_tau[:], OP.max, absv=True)
            VTS(sb_invt[:], sb_invt[:], 0.1, None, OP.add)
            VRC(sb_invt[:], sb_invt[:])
            val["vini"] = VTS(sb_ninv[:], sb_invt[:], -1.0, None, OP.mult)
            v.wait_ge(sINI, 2)
            val["scT"] = VTT(sb_cT[:], p_init[:, 8:12], sb_b1T[:], OP.add)

        @blk.tensor
        def _(t):
            t.wait_ge(semGi, 64)
            t.wait_ge(semC, 64)
            for dc in range(8):
                t.matmul(p_init[:, dc:dc + 1], sb_Sbf[:, dc * P:(dc + 1) * P],
                         c_tri[0:NS, 127:128], start=True, stop=True)
            for dc in range(8):
                ins = t.transpose(p_ini_b[:, dc * NS:(dc + 1) * NS],
                                  sb_Sbf[:, dc * P:(dc + 1) * P], c_eye[0:NS, 0:NS])
            ins.then_inc(sINI, 1)
            t.wait_ge(sSVG, V("vini", 0))
            w1bv = sb_w1b[:].rearrange("p (c j) -> p c j", j=DH)
            for qc in range(4):
                for dc in range(8):
                    ins = t.matmul(p_init[:, 8 + qc:9 + qc],
                                   w1bv[:, dc, qc * P:(qc + 1) * P],
                                   sb_smT[:, dc:dc + 1], start=(dc == 0), stop=(dc == 7))
            ins.then_inc(sINI, 1)

        es1i.close()
        pcs.append(es1.enter_context(nc.psum_tensor("pcs2", [P, D], F32)))

        @blk.vector
        def _(v):
            v.wait_ge(semC, 64)
            for g in range(NG):
                v.wait_ge(semX[g], 16)
                for j in range(4 * g, 4 * g + 4):
                    VBN(bn6x[:, j * 12:j * 12 + 6], sb_x[:, j * D:j * D + DH])
                    VBN(bn6x[:, j * 12 + 6:j * 12 + 12], sb_x[:, j * D + DH:(j + 1) * D])
                val["cA", g] = combine(SM_MX, SM_VX, bn6x, 4 * g, 4, None)
                v.wait_ge(sSSG, V(("sq1", g), 0))
                val["pA", g] = post(SM_S1, SM_B1, SM_G1, SM_MX, 4 * g, 4)
                for pr in (2 * g, 2 * g + 1):
                    if pr >= 1:
                        v.wait_ge(sSSG, V(("sq2", pr - 1), 0))
                        val["pC", pr - 1] = post(SM_S2, SM_B2, SM_G2, SM_MC, 2 * (pr - 1), 2, sco=44)
                    v.wait_ge(sCS, 2 * pr + 2)
                    for j in (2 * pr, 2 * pr + 1):
                        VBN(bn6c[:, j * 12:j * 12 + 6], pcs[j % 3][:, 0:DH])
                        val["bn6c", pr] = VBN(bn6c[:, j * 12 + 6:j * 12 + 12], pcs[j % 3][:, DH:D])
                    val["cC", pr] = combine(SM_MC, SM_VC, bn6c, 2 * pr, 2, c_e2[:, 2 * pr:2 * pr + 2])
            v.wait_ge(sSSG, V(("sq2", NPAIR - 1), 0))
            val["pC", NPAIR - 1] = post(SM_S2, SM_B2, SM_G2, SM_MC, 2 * (NPAIR - 1), 2, sco=44)

        @blk.scalar
        def _(sc):
            sc.preload_act_table(AF.Sqrt)
            sc.wait_ge(sINI, 2)
            pass
            for j in range(NT + 3):
                if j >= 2 and j % 2 == 0 and (j - 2) // 2 < NPAIR:
                    pr = (j - 2) // 2
                    sc.wait_ge(sSVG, V(("cC", pr), 0))
                    val["sq2", pr] = SACT(SC(SM_G2, 2 * pr, 2), SC(SM_VC, 2 * pr, 2), AF.Sqrt)
                if j >= 1 and j % 2 == 0 and (j // 2 - 1) < NT // 2:
                    sb_i = j // 2 - 1
                    sc.wait_ge(sBS, sb_i + 1)
                    val["cv", sb_i] = SACT(carry[:], ptot[0:1, :], AF.Identity)
                if j % 4 == 0 and j < NT:
                    g = j // 4
                    sc.wait_ge(sSVG, V(("cA", g), 0))
                    val["sq1", g] = SACT(SC(SM_G1, 4 * g, 4), SC(SM_VX, 4 * g, 4), AF.Sqrt)
                if j < NT:
                    sc.wait_ge(sSVG, V(("pA", j // 4), 0))
                    if j >= 4:
                        sc.wait_ge(sCS, j - 3)
                        sc.wait_ge(sBS, (j - 2) // 2)
                    val["x1", j] = SACT(x1b[:, (j % 4) * D:(j % 4 + 1) * D],
                                        sb_x[:, j * D:(j + 1) * D], AF.Identity,
                                        bias=SC(SM_B1, j), scale=SC(SM_S1, j))
                if j >= 2 and (j - 2) < NT:
                    jj = j - 2
                    sc.wait_ge(sSVG, V(("pC", jj // 2), 0))
                    if jj >= 3:
                        sc.wait_ge(sX2, jj - 2)
                    val["an", jj] = SACT(anb[:, (jj % 3) * D:(jj % 3 + 1) * D],
                                         pcs[jj % 3][:], AF.Identity,
                                         bias=SC(SM_B2, jj), scale=SC(SM_S2, jj))

        @blk.tensor
        def _(t):
            t.wait_ge(sSVG, V("scT", 0))
            for j in range(NT):
                t.wait_ge(sSSG, V(("x1", j), 0))
                if j >= 2:
                    t.wait_ge(sSSG, V(("cv", j // 2 - 1), 0))
                if j >= 3:
                    t.wait_ge(sSVG, V(("bn6c", (j - 3) // 2), 0))
                    t.wait_ge(sSSG, V(("an", j - 3), 0))
                b = pcs[j % 3]
                x1t = x1b[:, (j % 4) * D:(j % 4 + 1) * D]
                x1p = x1b[:, ((j - 1) % 4) * D:((j - 1) % 4 + 1) * D]
                for h in range(2):
                    ins = t.matmul(b[:, h * DH:(h + 1) * DH], c_tri[:],
                                   x1t[:, h * DH:(h + 1) * DH], start=True, stop=(j == 0))
                if j % 2 == 1:
                    for h in range(2):
                        ins = t.matmul(b[:, h * DH:(h + 1) * DH], c_one[:],
                                       x1p[:, h * DH:(h + 1) * DH], start=False, stop=(j == 1))
                if j >= 2:
                    for h in range(2):
                        ins = t.matmul(b[:, h * DH:(h + 1) * DH], c_tri[0:1, :],
                                       carry[:, h * DH:(h + 1) * DH], start=False, stop=True)
                ins.then_inc(sCS, 1)
                for h in range(2):
                    ins = t.matmul(ptot[0:1, h * DH:(h + 1) * DH], c_tri[:, 127:128],
                                   x1t[:, h * DH:(h + 1) * DH],
                                   start=(j % 2 == 0), stop=(j % 2 == 1 and j < 2))
                if j % 2 == 1 and j >= 2:
                    for h in range(2):
                        ins = t.matmul(ptot[0:1, h * DH:(h + 1) * DH], c_tri[0:1, 0:1],
                                       carry[:, h * DH:(h + 1) * DH], start=False, stop=True)
                if j % 2 == 1:
                    ins.then_inc(sBS, 1)

        @blk.gpsimd
        def _(g):
            for j in range(NT):
                g.wait_ge(sSSG, V(("an", j), 0))
                g.tensor_tensor(sb_x[:, j * D:(j + 1) * D], anb[:, (j % 3) * D:(j % 3 + 1) * D],
                                sb_x[:, j * D:(j + 1) * D], op=OP.add).then_inc(sX2, 1)

        if stage == 1:
            @blk.sync
            def _(s):
                for gg in range(8):
                    s.wait_ge(sX2, 4 * gg + 4)
                    s.dma_start(out=out_r[:, 4 * gg:4 * gg + 4, :],
                                in_=sb_x[:].rearrange("p (j c) -> p j c", c=D)[:, 4 * gg:4 * gg + 4, :]
                                ).then_inc(sOUT, 16)
                s.wait_ge(sOUT, 8 * 16)
            return nc, val

        es1.close()

        es2 = ExitStack()
        sb2 = lambda name, shp, dt=F32: es2.enter_context(nc.sbuf_tensor(name, shp, dt))
        whhT = sb2("whhT", [P, G3], BF16)
        dn_f = sb2("dn_f", [NS, 2])
        es2b = ExitStack()
        sb2b = lambda name, shp, dt=F32: es2b.enter_context(nc.sbuf_tensor(name, shp, dt))
        x3b = sb2b("x3b", [P, 2 * D], BF16)
        x3t4 = sb2b("x3t4", [P, 4 * D], BF16)
        htb4 = sb2b("htb4", [P, 4 * DH], BF16)
        sb_L = sb2b("sb_L", [P, NT * NS])
        sb_E = sb2b("sb_E", [P, 2 * NS])
        m8b = sb2b("m8b", [P, 16])
        aub = sb2b("aub", [P, 2 * NS])
        zcol = sb2b("zcol", [P, 8])
        x2bf = sb2b("x2bf", [P, 2 * D], BF16)
        if stage == 2:
            dbg_sb = sb2b("dbg_sb", [NS, D + 2], BF16)
        es2p = ExitStack()
        p_xt = [es2p.enter_context(nc.psum_tensor("p_xt0", [P, D], BF16)),
                es2p.enter_context(nc.psum_tensor("p_xt1", [P, D], BF16))]
        p_h4 = [es2p.enter_context(nc.psum_tensor(f"p_h4{i}", [P, DH], F32))
                for i in range(4)]
        p_lg = [es2p.enter_context(nc.psum_tensor("p_lg0", [P, DH], F32)),
                es2p.enter_context(nc.psum_tensor("p_lg1", [P, DH], F32))]

        @blk.sync
        def _(s):
            s.wait_ge(sCS, NT)
            s.wait_ge(sX2, NT)
            s.wait_ge(sSVG, V(("pC", NPAIR - 1), 0))
            s.wait_ge(sSSG, V(("an", NT - 1), 0))
            s.wait_ge(semGh, 16)
            for dc in range(8):
                if dc >= 1:
                    s.wait_ge(sGHc, dc)
                s.dma_start_transpose(whhT[:],
                                      whh_bf[:, dc * P:(dc + 1) * P]).then_inc(semWT[dc], 16)

        @blk.vector
        def _(v):
            def xt4(j):
                if j == 0:
                    v.wait_ge(sX2, NT)
                v.wait_ge(sTT, j + 1)
                v.wait_ge(sHH, j // 4)
                val["xt4", j] = VCP(
                    x3t4[:].rearrange("p (c t k) -> p c t k", t=4, k=P)[:, :, j % 4, :],
                    p_xt[j % 2][:].rearrange("p (c k) -> p c k", k=P))

            for g in range(NG):
                v.wait_ge(sX2, 4 * g + 4)
                for j in range(4 * g, 4 * g + 4):
                    VBN(bn6x[:, j * 12:j * 12 + 6], sb_x[:, j * D:j * D + DH])
                    VBN(bn6x[:, j * 12 + 6:j * 12 + 12], sb_x[:, j * D + DH:(j + 1) * D])
                val["cA3", g] = combine(SM_M3, SM_V3, bn6x, 4 * g, 4, None)
                if g >= 1:
                    v.wait_ge(sSSG, V(("sq3", g - 1), 0))
                    val["pA3", g - 1] = post(SM_S3, SM_B3, SM_G3, SM_M3, 4 * (g - 1), 4)
                    for j in range(4 * (g - 1), 4 * g):
                        xt4(j)
            v.wait_ge(sSSG, V(("sq3", NG - 1), 0))
            val["pA3", NG - 1] = post(SM_S3, SM_B3, SM_G3, SM_M3, 4 * (NG - 1), 4)
            val["nmu3"] = VCP(b3bf[:], stat[:, SM_B3 * NT:SM_B3 * NT + NT])
            for j in range(4 * (NG - 1), NT):
                xt4(j)

        @blk.scalar
        def _(sc):
            for G in range(NG + 2):
                if G < NG:
                    sc.preload_act_table(AF.Sqrt)
                    sc.wait_ge(sSVG, V(("cA3", G), 0))
                    val["sq3", G] = SACT(SC(SM_G3, 4 * G, 4), SC(SM_V3, 4 * G, 4), AF.Sqrt)
                    sc.preload_act_table(AF.Gelu)
                    for t4 in range(4):
                        j = 4 * G + t4
                        sc.wait_ge(sSVG, V(("pA3", G), 0))
                        if j >= 2:
                            sc.wait_ge(sTT, j - 1)
                        val["x3", j] = SACT(x3b[:, (j % 2) * D:(j % 2 + 1) * D],
                                            sb_x[:, j * D:(j + 1) * D], AF.Identity,
                                            bias=SC(SM_B3, j), scale=SC(SM_S3, j))
                if 0 <= G - 1 < NG:
                    H = G - 1
                    sc.wait_ge(sHH, H + 1)
                    if H >= 1:
                        sc.wait_ge(sWW, 4 * (H - 1) + 4)
                    for jc in range(4):
                        ins_g = SACT(htb4[:, jc * DH:(jc + 1) * DH], p_h4[jc][:],
                                     AF.Gelu, bias=sb_cT[:, jc:jc + 1])
                    val["gel", H] = ins_g
                if G >= 3 and (G - 3) % 2 == 0 and (G - 3) // 2 < 4:
                    bb = (G - 3) // 2
                    sc.wait_ge(sWW, 8 * bb + 8)
                    val["lg", bb] = SACT(sb_L[:, bb * 8 * NS:(bb + 1) * 8 * NS],
                                         p_lg[bb % 2][:], AF.Identity)

        @blk.tensor
        def _(t):
            t.wait_ge(sSVG, V(("bn6c", NPAIR - 1), 0))
            t.wait_ge(sSSG, V(("an", NT - 1), 0))
            t.wait_ge(sSSG, V(("cv", NT // 2 - 1), 0))
            t.wait_ge(sX2, NT)
            t.wait_ge(semGa, 16)
            t.wait_ge(semGb, 32)
            t.wait_ge(sSVG, V("scT", 0))
            w1av = sb_w1a[:].rearrange("p (c j) -> p c j", j=DH)
            w2v = sb_w2[:].rearrange("p (c j) -> p c j", j=NS)
            for G in range(NG + 2):
                if 0 <= G - 1 < NG:
                    H = G - 1
                    t.wait_ge(sSVG, V(("xt4", 4 * H + 3), 0))
                    if H >= 1:
                        t.wait_ge(sSSG, V(("gel", H - 1), 0))
                    for dc in range(8):
                        for jc in range(4):
                            ins = t.matmul(p_h4[jc][:, 0:DH],
                                           w1av[:, dc, jc * P:(jc + 1) * P],
                                           x3t4[:, dc * DH:(dc + 1) * DH],
                                           start=(dc == 0), stop=(dc == 7))
                    ins.then_inc(sHH, 1)
                if 0 <= G - 2 < NG:
                    H2 = G - 2
                    t.wait_ge(sSSG, V(("gel", H2), 0))
                    if H2 >= 4 and H2 % 2 == 0:
                        t.wait_ge(sSSG, V(("lg", (H2 - 4) // 2), 0))
                    for t4 in range(4):
                        j = 4 * H2 + t4
                        plg = p_lg[(j // 8) % 2]
                        for jc in range(4):
                            t.matmul(plg[:, (j % 8) * NS:(j % 8 + 1) * NS],
                                     htb4[:, jc * DH + t4 * P:jc * DH + (t4 + 1) * P],
                                     w2v[:, jc, :],
                                     start=(j % 8 == 0 and jc == 0), stop=False)
                        ins = t.matmul(plg[:, (j % 8) * NS:(j % 8 + 1) * NS],
                                       c_one[0:1, :], sb_b2row[:],
                                       start=False, stop=(j % 8 == 7))
                        ins.then_inc(sWW, 1)
                if G < NG:
                    for t4 in range(4):
                        j = 4 * G + t4
                        t.wait_ge(sSSG, V(("x3", j), 0))
                        if j >= 2:
                            t.wait_ge(sSVG, V(("xt4", j - 2), 0))
                        x3v = x3b[:, (j % 2) * D:(j % 2 + 1) * D]
                        for dc in range(8):
                            ins = t.transpose(p_xt[j % 2][:, dc * P:(dc + 1) * P],
                                              x3v[:, dc * P:(dc + 1) * P], c_eye[:])
                        ins.then_inc(sTT, 1)

        es2p.close()
        es3p = ExitStack()
        p_si = es3p.enter_context(nc.psum_tensor("p_si", [NS, D], F32))
        p_d2 = es3p.enter_context(nc.psum_tensor("p_d2", [NS, 1], F32))
        p_at = es3p.enter_context(nc.psum_tensor("p_at", [NS, 4 * P], BF16))
        p_gh3 = es3p.enter_context(nc.psum_tensor("p_gh3", [P, 24 * NS], F32))

        @blk.vector
        def _(v):
            def altail(j):
                lt = sb_L[:, j * NS:(j + 1) * NS]
                v.wait_ge(sSSG, V(("exp", j), 0))
                ev_ = sb_E[:, (j % 2) * NS:(j % 2 + 1) * NS]
                au = aub[:, (j % 2) * NS:(j % 2 + 1) * NS]
                val["au", j] = VSTT(au, lt, m8b[:, (j % 2) * 8 + 7:(j % 2) * 8 + 8], ev_,
                                    OP.is_ge, OP.mult, accum=zcol[:, (j % 2) * 4:(j % 2) * 4 + 1])
                VRC(zcol[:, (j % 2) * 4 + 2:(j % 2) * 4 + 3], zcol[:, (j % 2) * 4:(j % 2) * 4 + 1])
                if j >= 2:
                    v.wait_ge(sP3T, j - 1)
                val["alf", j] = VTS(sb_alpha[:, j * NS:(j + 1) * NS], au,
                                    zcol[:, (j % 2) * 4 + 2:(j % 2) * 4 + 3], None, OP.mult)
                if j % 4 == 3:
                    v.wait_ge(sP3T, j + 1)
                    val["ate", j // 4] = VCP(sb_alphaT[:, (j - 3) * P:(j + 1) * P], p_at[0:NS, :])

            for j in range(NT):
                v.wait_ge(sSSG, V(("lg", j // 8), 0))
                if j >= 2:
                    v.wait_ge(sSSG, V(("exp", j - 2), 0))
                val["m8", j] = VMX(m8b[:, (j % 2) * 8:(j % 2) * 8 + 8], sb_L[:, j * NS:(j + 1) * NS])
                val["eb", j] = VTT(zcol[:, (j % 2) * 4 + 1:(j % 2) * 4 + 2],
                                   m8b[:, (j % 2) * 8:(j % 2) * 8 + 1], sb_ninv[:], OP.mult)
                if j >= 1:
                    altail(j - 1)
            altail(NT - 1)
            v.wait_ge(sGHc, 8)
            val["ghe"] = VCP(sb_gh[:], p_gh3[:])

        @blk.scalar
        def _(sc):
            sc.preload_act_table(AF.Exp)
            for j in range(NT):
                sc.wait_ge(sSVG, V(("eb", j), 0))
                if j >= 2:
                    sc.wait_ge(sSVG, V(("au", j - 2), 0))
                val["exp", j] = SACT(sb_E[:, (j % 2) * NS:(j % 2 + 1) * NS],
                                     sb_L[:, j * NS:(j + 1) * NS], AF.Exp,
                                     bias=zcol[:, (j % 2) * 4 + 1:(j % 2) * 4 + 2],
                                     scale=sb_invt[:])
                if j >= 2:
                    sc.wait_ge(sP3T, j - 1)
                val["x2b", j] = SACT(x2bf[:, (j % 2) * D:(j % 2 + 1) * D],
                                     sb_x[:, j * D:(j + 1) * D], AF.Identity,
                                     scale=SC(SM_S3, j))

        @blk.tensor
        def _(t):
            t.wait_ge(sSSG, V(("lg", 3), 0))
            t.wait_ge(sSVG, V(("xt4", NT - 1), 0))
            t.wait_ge(sSSG, V(("gel", NG - 1), 0))
            t.wait_ge(semGz, 16)
            for dc in range(8):
                t.wait_ge(semWT[dc], 16)
                for gc in range(24):
                    ins = t.matmul(p_gh3[:, gc * NS:(gc + 1) * NS],
                                   whhT[:, gc * P:(gc + 1) * P],
                                   sb_St[:, dc * NS:(dc + 1) * NS],
                                   start=(dc == 0 and gc % 8 == 0), stop=False)
                if dc == 7:
                    for gc in range(24):
                        ins = t.matmul(p_gh3[:, gc * NS:(gc + 1) * NS],
                                       sb_bz[:, gc * P:(gc + 1) * P],
                                       c_one[0:1, 0:NS], start=False, stop=(gc % 8 == 7))
                ins.then_inc(sGHc, 1)
            for j in range(NT):
                t.wait_ge(sSVG, V(("alf", j), 0))
                t.wait_ge(sSSG, V(("x2b", j), 0))
                alv = sb_alpha[:, j * NS:(j + 1) * NS]
                x2v = x2bf[:, (j % 2) * D:(j % 2 + 1) * D]
                for h in range(2):
                    t.matmul(p_si[0:NS, h * DH:(h + 1) * DH], alv, x2v[:, h * DH:(h + 1) * DH],
                             start=(j == 0), stop=(j == NT - 1))
                t.matmul(p_d2[0:NS, 0:1], alv, b3bf[:, j:j + 1],
                         start=(j == 0), stop=(j == NT - 1))
                if j % 4 == 0 and j >= 4:
                    t.wait_ge(sSVG, V(("ate", j // 4 - 1), 0))
                ins = t.transpose(p_at[0:NS, (j % 4) * P:(j % 4 + 1) * P], alv, c_eye[:])
                ins.then_inc(sP3T, 1)


        if stage >= 3:
            es2b.close()
            es4s = ExitStack()
            sb4 = lambda name, shp, dt=F32: es4s.enter_context(nc.sbuf_tensor(name, shp, dt))
            dn_f4 = sb4("dn_f4", [NS, 2])
            sb_d2c = sb4("sb_d2c", [NS, 1])
            wst = sb4("wst", [P, 4 * D], BF16)
            ycol = sb4("ycol", [NS, 8])
            lnbw = sb4("lnbw", [1, D], BF16)
            sb_sv = sb4("sb_sv", [NS, D], BF16)
            svT = sb4("svT", [P, 8 * NS], BF16)
            sb_sv2 = sb4("sb_sv2", [NS, D], BF16)
            es4s2 = ExitStack()
            sb42 = lambda name, shp, dt=F32: es4s2.enter_context(nc.sbuf_tensor(name, shp, dt))
            si_bf = sb42("si_bf", [NS, D], BF16)
            siT = sb42("siT", [P, 8 * NS], BF16)
            tn1 = sb42("tn1", [P, DH])
            tn2 = sb42("tn2", [P, DH])
            tn3 = sb42("tn3", [P, DH])
            SnT = sb42("SnT", [P, 8 * NS], BF16)
            qT = sb42("qT", [P, 2 * NS], BF16)
            kT = sb42("kT", [P, 2 * NS], BF16)
            sb_att = sb42("sb_att", [NS, NS])
            attbf = sb42("attbf", [NS, NS], BF16)
            attT = sb42("attT", [NS, NS], BF16)
            sb_v = sb42("sb_v", [NS, D], BF16)
            Snsl = sb42("Snsl", [NS, D], BF16)
            sb_yn = sb42("sb_yn", [NS, D], BF16)
            ynT = sb42("ynT", [P, 8 * NS], BF16)
            semL1 = sem("semL1")
            semW4 = [sem(f"semW4{c}") for c in range(8)]
            semW5 = [sem(f"semW5{c}") for c in range(8)]
            semWS = [sem(f"semWS{c}") for c in range(8)]
            semW2 = [sem(f"semW2{c}") for c in range(8)]
            semW3 = [sem(f"semW3{c}") for c in range(8)]
            sGIc = sem("sGIc")
            sWSc = sem("sWSc")
            sX5 = sem("sX5")
            sX5o = sem("sX5o")

            @blk.gpsimd
            def _(g):
                g.wait_ge(sGHc, 8)
                g.dma_start(out=sb_bz[:], in_=w_d["bih"].rearrange("(o j) -> o j", o=1)).then_inc(semGz, 16)
                g.dma_start(out=sb_bq[:].rearrange("p c -> p c"),
                            in_=w_d["bq"].rearrange("(c p) -> p c", p=P), allow_slow_non_contiguous=True).then_inc(semL1, 16)
                g.dma_start(out=sb_bk[:], in_=w_d["bk"].rearrange("(c p) -> p c", p=P), allow_slow_non_contiguous=True).then_inc(semL1, 16)
                g.dma_start(out=sb_brow[:], in_=w_d["bv"].rearrange("(o j) -> o j", o=1)).then_inc(semL1, 16)
                g.dma_start(out=sb_lng[:], in_=w_d["ln_g"].rearrange("(c p) -> p c", p=P), allow_slow_non_contiguous=True).then_inc(semL1, 16)
                g.dma_start(out=sb_lnbT[:], in_=w_d["ln_b"].rearrange("(c p) -> p c", p=P), allow_slow_non_contiguous=True).then_inc(semL1, 16)
                g.wait_ge(sGIc, 11)
                g.dma_start(out=sb_brow[:], in_=w_d["bvp"].rearrange("(o j) -> o j", o=1)).then_inc(semGz, 16)
                g.wait_ge(sGIc, 15)
                g.dma_start(out=sb_brow[:], in_=w_d["bo"].rearrange("(o j) -> o j", o=1)).then_inc(semGz, 16)

            @blk.sync
            def _(s):
                s.wait_ge(sP3T, NT)
                s.wait_ge(semGq, 16 * 5)
                s.wait_ge(sGHc, 8)
                s.wait_ge(semGw, 16)
                for dc in range(8):
                    if dc >= 1:
                        s.wait_ge(sGIc, dc)
                    s.dma_start_transpose(whhT[:], wih_bf[:, dc * P:(dc + 1) * P]).then_inc(semWT[dc], 16)
                for mi, (w_hbm, sems, wdt) in enumerate((
                        (wq_bf, semW4, DQ), (wk_bf, semW5, DQ), (wv_bf, semWS, D),
                        (wvp_bf, semW2, D), (wo_bf, semW3, D))):
                    for dc in range(8):
                        k = mi * 8 + dc
                        if k >= 4:
                            s.wait_ge(sWSc, k - 3)
                        s.dma_start(out=wst[:, (dc % 4) * D:(dc % 4) * D + wdt],
                                    in_=w_hbm.rearrange("(c p) q -> p c q", p=P)[:, dc, :]).then_inc(sems[dc], 16)

            @blk.vector
            def _(v):
                v.wait_ge(sP3T, NT)
                VRED(dn_f4[:, 0:1], sb_alphaT[:], OP.add)
                VTS(dn_f4[:, 1:2], dn_f4[:, 0:1], 1e-8, None, OP.add)
                VRC(dn_f4[:, 1:2], dn_f4[:, 1:2])
                VCP(sb_d2c[:], p_d2[0:NS, :])
                val["sie"] = VTS(si_bf[:], p_si[0:NS, :], sb_d2c[:], dn_f4[:, 1:2], OP.add, OP.mult)

            es3p.close()
            es4pA = ExitStack()
            p4a = es4pA.enter_context(nc.psum_tensor("p4a", [P, D], BF16))
            es4pB = ExitStack()
            p_qk = es4pB.enter_context(nc.psum_tensor("p_qk", [P, DH], F32))
            p_sc = es4pB.enter_context(nc.psum_tensor("p_sc", [NS, NS], F32))
            p_gi = es4pB.enter_context(nc.psum_tensor("p_gi", [P, 24 * NS], F32))

            @blk.tensor
            def _(t):
                t.wait_ge(sSVG, V("sie", 0))
                t.wait_ge(sSVG, V(("ate", NT // 4 - 1), 0))
                for dc in range(8):
                    ins = t.transpose(p4a[:, dc * NS:(dc + 1) * NS],
                                      si_bf[:, dc * P:(dc + 1) * P], c_eye[0:NS, 0:NS])
                ins.then_inc(sP3T, 1)

            @blk.vector
            def _(v):
                v.wait_ge(sP3T, NT + 1)
                val["siT"] = VCP(siT[:], p4a[:, 0:8 * NS])

            @blk.tensor
            def _(t):
                t.wait_ge(sSVG, V("siT", 0))
                t.wait_ge(semGz, 32)
                for dc in range(8):
                    t.wait_ge(semWT[dc], 32)
                    for gc in range(24):
                        ins = t.matmul(p_gi[:, gc * NS:(gc + 1) * NS],
                                       whhT[:, gc * P:(gc + 1) * P],
                                       siT[:, dc * NS:(dc + 1) * NS],
                                       start=(dc == 0 and gc % 8 == 0), stop=False)
                    if dc == 7:
                        for gc in range(24):
                            ins = t.matmul(p_gi[:, gc * NS:(gc + 1) * NS],
                                           sb_bz[:, gc * P:(gc + 1) * P],
                                           c_one[0:1, 0:NS], start=False, stop=(gc % 8 == 7))
                    ins.then_inc(sGIc, 1)

            @blk.vector
            def _(v):
                v.wait_ge(sGIc, 8)
                val["rp"] = VTT(tn1[:], p_gi[:, 0:DH], sb_gh[:, 0:DH], OP.add)
                val["zp"] = VTT(tn3[:], p_gi[:, DH:2 * DH], sb_gh[:, DH:2 * DH], OP.add)
                v.wait_ge(sSSG, V("rz", 0))
                VTT(tn2[:], tn1[:], sb_gh[:, 2 * DH:3 * DH], OP.mult)
                val["tn2"] = VTT(tn2[:], p_gi[:, 2 * DH:3 * DH], tn2[:], OP.add)
                v.wait_ge(sSSG, V("tanh", 0))
                VTT(tn1[:], sb_St[:], tn2[:], OP.subtract)
                VTT(tn1[:], tn3[:], tn1[:], OP.mult)
                val["snt"] = VTT(SnT[:], tn2[:], tn1[:], OP.add)

            @blk.scalar
            def _(sc):
                sc.preload_act_table(AF.Sigmoid)
                sc.wait_ge(sSVG, V("rp", 0))
                SACT(tn1[:], tn1[:], AF.Sigmoid)
                sc.wait_ge(sSVG, V("zp", 0))
                val["rz"] = SACT(tn3[:], tn3[:], AF.Sigmoid)
                sc.wait_ge(sSVG, V("tn2", 0))
                val["tanh"] = SACT(tn2[:], tn2[:], AF.Tanh)

            @blk.tensor
            def _(t):
                t.wait_ge(sSVG, V("snt", 0))
                t.wait_ge(semL1, 80)
                for mi, sems in ((0, semW4), (1, semW5)):
                    for dc in range(8):
                        t.wait_ge(sems[dc], 16)
                        wc = wst[:, (dc % 4) * D:(dc % 4) * D + DQ]
                        for qc in range(2):
                            ins = t.matmul(p_qk[:, mi * 2 * NS + qc * NS:mi * 2 * NS + (qc + 1) * NS],
                                           wc[:, qc * P:(qc + 1) * P],
                                           SnT[:, dc * NS:(dc + 1) * NS],
                                           start=(mi == 0 and dc == 0 and qc == 0),
                                           stop=(mi == 1 and dc == 7 and qc == 1))
                        ins.then_inc(sWSc, 1)

            @blk.scalar
            def _(sc):
                sc.wait_ge(sWSc, 16)
                for qc in range(2):
                    SACT(qT[:, qc * NS:(qc + 1) * NS], p_qk[:, qc * NS:(qc + 1) * NS],
                         AF.Identity, bias=sb_bq[:, qc:qc + 1])
                for qc in range(2):
                    ins_v = SACT(kT[:, qc * NS:(qc + 1) * NS], p_qk[:, 2 * NS + qc * NS:2 * NS + (qc + 1) * NS],
                                 AF.Identity, bias=sb_bk[:, qc:qc + 1])
                val["qke"] = ins_v

            @blk.tensor
            def _(t):
                t.wait_ge(sSSG, V("qke", 0))
                for qc in range(2):
                    ins = t.matmul(p_sc[0:NS, :], qT[:, qc * NS:(qc + 1) * NS],
                                   kT[:, qc * NS:(qc + 1) * NS],
                                   start=(qc == 0), stop=(qc == 1))
                ins.then_inc(sGIc, 1)

            @blk.vector
            def _(v):
                v.wait_ge(sGIc, 9)
                VRED(ycol[:, 0:1], p_sc[0:NS, :], OP.max)
                val["smx"] = VTS(ycol[:, 1:2], ycol[:, 0:1], -1.0 / 16.0, None, OP.mult)
                v.wait_ge(sSSG, V("attx", 0))
                VRC(ycol[:, 3:4], ycol[:, 2:3])
                val["attn"] = VTS(attbf[:], sb_att[:], ycol[:, 3:4], None, OP.mult)

            @blk.scalar
            def _(sc):
                sc.add_instruction(mybir.InstLoadActFuncSet(
                    name=nc.get_next_instruction_name(),
                    act_func_set_id=SET_LNEXP, ins=[], outs=[]))
                sc.wait_ge(sSVG, V("smx", 0))
                val["attx"] = SACT(sb_att[:], p_sc[0:NS, :], AF.Exp,
                                   bias=ycol[:, 1:2], scale=1.0 / 16.0, accum=ycol[:, 2:3])

            @blk.tensor
            def _(t):
                t.wait_ge(sSVG, V("attn", 0))
                t.transpose(p4a[0:NS, 0:NS], attbf[:], c_eye[0:NS, 0:NS]).then_inc(sGIc, 1)

            @blk.vector
            def _(v):
                v.wait_ge(sGIc, 10)
                val["attT"] = VCP(attT[:], p4a[0:NS, 0:NS])

            es4pB.close()
            es4pC = ExitStack()
            p_v = es4pC.enter_context(nc.psum_tensor("p_v", [NS, D], F32))
            p_scm = es4pC.enter_context(nc.psum_tensor("p_scm", [NS, D], F32))

            @blk.tensor
            def _(t):
                t.wait_ge(sSVG, V("attT", 0))
                for dc in range(8):
                    t.wait_ge(semWS[dc], 16)
                    wc = wst[:, (dc % 4) * D:(dc % 4 + 1) * D]
                    for h in range(2):
                        ins = t.matmul(p_v[0:NS, h * DH:(h + 1) * DH],
                                       SnT[:, dc * NS:(dc + 1) * NS], wc[:, h * DH:(h + 1) * DH],
                                       start=(dc == 0), stop=False)
                    ins.then_inc(sWSc, 1)
                t.wait_ge(semL1, 80)
                for h in range(2):
                    ins = t.matmul(p_v[0:NS, h * DH:(h + 1) * DH], c_one[0:1, 0:NS],
                                   sb_brow[:, h * DH:(h + 1) * DH], start=False, stop=True)
                ins.then_inc(sGIc, 1)

            @blk.scalar
            def _(sc):
                sc.wait_ge(sGIc, 11)
                val["ve"] = SACT(sb_v[:], p_v[0:NS, :], AF.Identity)

            @blk.tensor
            def _(t):
                t.wait_ge(sSSG, V("ve", 0))
                for dc in range(8):
                    ins2 = t.transpose(p4a[0:NS, dc * P:(dc + 1) * P],
                                       SnT[:, dc * NS:(dc + 1) * NS], c_eye[:])
                ins2.then_inc(sGIc, 1)

            @blk.vector
            def _(v):
                v.wait_ge(sGIc, 12)
                val["snsl"] = VCP(Snsl[:], p4a[0:NS, 0:D])

            @blk.tensor
            def _(t):
                t.wait_ge(sSVG, V("snsl", 0))
                for h in range(2):
                    t.matmul(p_scm[0:NS, h * DH:(h + 1) * DH], attT[:],
                             sb_v[:, h * DH:(h + 1) * DH], start=True, stop=False)
                for h in range(2):
                    ins = t.matmul(p_scm[0:NS, h * DH:(h + 1) * DH], c_eye[0:NS, 0:NS],
                                   Snsl[:, h * DH:(h + 1) * DH], start=False, stop=True)
                ins.then_inc(sGIc, 1)

            @blk.vector
            def _(v):
                v.wait_ge(sGIc, 13)
                VBN(bn6x[0:NS, 0:6], p_scm[0:NS, 0:DH])
                VBN(bn6x[0:NS, 6:12], p_scm[0:NS, DH:D])
                view = bn6x[0:NS, 0:12].rearrange("p (g k) -> p g k", k=3)
                mus = view[:, :, 1]
                cvs = view[:, :, 2]
                VRED(ycol[:, 4:5], mus, OP.add)
                VTT(tmp4[0:NS, 0:4], mus, mus, OP.mult)
                VRED(ycol[:, 5:6], cvs, OP.add)
                VRED(tmp4[0:NS, 4:5], tmp4[0:NS, 0:4], OP.add)
                VTT(tmp4[0:NS, 5:6], ycol[:, 4:5], ycol[:, 4:5], OP.mult)
                VTS(tmp4[0:NS, 6:7], ycol[:, 5:6], 1.0 / 1024.0, EPS, OP.mult, OP.add)
                VSTT(tmp4[0:NS, 7:8], tmp4[0:NS, 4:5], 0.25, tmp4[0:NS, 6:7], OP.mult, OP.add)
                val["yvar"] = VSTT(ycol[:, 6:7], tmp4[0:NS, 5:6], -1.0 / 16.0, tmp4[0:NS, 7:8], OP.mult, OP.add)
                v.wait_ge(sSSG, V("yrs", 0))
                VTS(tmp4[0:NS, 8:9], ycol[:, 4:5], -0.25, None, OP.mult)
                val["yb"] = VTT(ycol[:, 7:8], tmp4[0:NS, 8:9], ycol[:, 2:3], OP.mult)

            @blk.scalar
            def _(sc):
                sc.wait_ge(sSVG, V("yvar", 0))
                SACT(tmp4[0:NS, 9:10], ycol[:, 6:7], AF.Ln)
                val["yrs"] = SACT(ycol[:, 2:3], tmp4[0:NS, 9:10], AF.Exp, scale=-0.5)
                sc.wait_ge(sSVG, V("yb", 0))
                val["yn"] = SACT(sb_yn[:], p_scm[0:NS, :], AF.Identity,
                                 bias=ycol[:, 7:8], scale=ycol[:, 2:3])

            es4pC.close()
            es4pD = ExitStack()
            p_sv = es4pD.enter_context(nc.psum_tensor("p_sv", [NS, D], F32))
            p_lnb = es4pD.enter_context(nc.psum_tensor("p_lnb", [1, D], F32))
            p_sv2 = es4pD.enter_context(nc.psum_tensor("p_sv2", [NS, D], F32))

            @blk.tensor
            def _(t):
                t.wait_ge(sSSG, V("yn", 0))
                t.wait_ge(sSVG, V("snsl", 0))
                for dc in range(8):
                    ins = t.transpose(p4a[:, dc * NS:(dc + 1) * NS],
                                      sb_yn[:, dc * P:(dc + 1) * P], c_eye[0:NS, 0:NS])
                ins.then_inc(sGIc, 1)

            @blk.vector
            def _(v):
                v.wait_ge(sGIc, 14)
                v.wait_ge(semL1, 80)
                for dc in range(8):
                    ins_v = VTS(ynT[:, dc * NS:(dc + 1) * NS], p4a[:, dc * NS:(dc + 1) * NS],
                                sb_lng[:, dc:dc + 1], None, OP.mult)
                val["ynT"] = ins_v

            @blk.tensor
            def _(t):
                t.wait_ge(sSVG, V("ynT", 0))
                t.wait_ge(semL1, 80)
                for dc in range(8):
                    t.wait_ge(semW2[dc], 16)
                    wc = wst[:, (dc % 4) * D:(dc % 4 + 1) * D]
                    for h in range(2):
                        t.matmul(p_sv[0:NS, h * DH:(h + 1) * DH],
                                 ynT[:, dc * NS:(dc + 1) * NS], wc[:, h * DH:(h + 1) * DH],
                                 start=(dc == 0), stop=False)
                        ins = t.matmul(p_lnb[0:1, h * DH:(h + 1) * DH],
                                       sb_lnbT[:, dc:dc + 1], wc[:, h * DH:(h + 1) * DH],
                                       start=(dc == 0), stop=False)
                    ins.then_inc(sWSc, 1)
                t.wait_ge(semGz, 48)
                for h in range(2):
                    ins = t.matmul(p_lnb[0:1, h * DH:(h + 1) * DH], c_tri[0:1, 0:1],
                                   sb_brow[:, h * DH:(h + 1) * DH], start=False, stop=True)
                ins.then_inc(sGIc, 1)

            @blk.scalar
            def _(sc):
                sc.wait_ge(sGIc, 15)
                val["lnbw"] = SACT(lnbw[:], p_lnb[0:1, :], AF.Identity)

            @blk.tensor
            def _(t):
                t.wait_ge(sSSG, V("lnbw", 0))
                for h in range(2):
                    ins = t.matmul(p_sv[0:NS, h * DH:(h + 1) * DH], c_one[0:1, 0:NS],
                                   lnbw[:, h * DH:(h + 1) * DH], start=False, stop=True)
                ins.then_inc(sGIc, 1)

            @blk.scalar
            def _(sc):
                sc.wait_ge(sGIc, 16)
                val["sve"] = SACT(sb_sv[:], p_sv[0:NS, :], AF.Identity)

            @blk.tensor
            def _(t):
                t.wait_ge(sSSG, V("sve", 0))
                for dc in range(8):
                    ins = t.transpose(p4a[:, dc * NS:(dc + 1) * NS],
                                      sb_sv[:, dc * P:(dc + 1) * P], c_eye[0:NS, 0:NS])
                ins.then_inc(sGIc, 1)

            @blk.vector
            def _(v):
                v.wait_ge(sGIc, 17)
                val["svT"] = VCP(svT[:], p4a[:, 0:8 * NS])

            @blk.tensor
            def _(t):
                t.wait_ge(sSVG, V("svT", 0))
                for dc in range(8):
                    t.wait_ge(semW3[dc], 16)
                    wc = wst[:, (dc % 4) * D:(dc % 4 + 1) * D]
                    for h in range(2):
                        ins = t.matmul(p_sv2[0:NS, h * DH:(h + 1) * DH],
                                       svT[:, dc * NS:(dc + 1) * NS], wc[:, h * DH:(h + 1) * DH],
                                       start=(dc == 0), stop=False)
                    ins.then_inc(sWSc, 1)
                t.wait_ge(semGz, 64)
                for h in range(2):
                    ins = t.matmul(p_sv2[0:NS, h * DH:(h + 1) * DH], c_one[0:1, 0:NS],
                                   sb_brow[:, h * DH:(h + 1) * DH], start=False, stop=True)
                ins.then_inc(sGIc, 1)

            @blk.scalar
            def _(sc):
                sc.wait_ge(sGIc, 18)
                val["sv2e"] = SACT(sb_sv2[:], p_sv2[0:NS, :], AF.Identity)

            if _DBG3:
                @blk.gpsimd
                def _(g):
                    g.wait_ge(sGIc, 18)
                    g.wait_ge(sSSG, V("sv2e", 0))
                    g.dma_start(out=dbg3[0:NS, 0:D], in_=si_bf[:]).then_inc(sDBG, 16)
                    g.dma_start(out=dbg3[:, D:D + DH], in_=SnT[:]).then_inc(sDBG, 16)
                    g.dma_start(out=dbg3[0:NS, 2 * D:2 * D + NS], in_=attbf[:]).then_inc(sDBG, 16)
                    g.dma_start(out=dbg3[0:NS, 3 * D:4 * D], in_=sb_v[:]).then_inc(sDBG, 16)
                    g.dma_start(out=dbg3[0:NS, 4 * D:5 * D], in_=sb_yn[:]).then_inc(sDBG, 16)
                    g.dma_start(out=dbg3[0:NS, 5 * D:6 * D], in_=sb_sv[:]).then_inc(sDBG, 16)
                    g.dma_start(out=dbg3[0:NS, 6 * D:7 * D], in_=sb_sv2[:]).then_inc(sDBG, 16)
                    g.dma_start(out=dbg3[:, 7 * D:7 * D + 24 * NS // 2], in_=sb_gh[:, 0:768]).then_inc(sDBG, 16)
                    g.wait_ge(sDBG, 128)
            es4s2.close()
            fnb = sb4("fnb", [P, 4 * D])
            es4pD.close()
            es4pA.close()
            es5p = ExitStack()
            p_f = [es5p.enter_context(nc.psum_tensor(f"p_f{i}", [P, D], F32))
                   for i in range(4)]

            @blk.tensor
            def _(t):
                t.wait_ge(sSSG, V("sv2e", 0))
                t.wait_ge(sSVG, V(("ate", NT // 4 - 1), 0))
                for j in range(NT):
                    if j >= 4:
                        t.wait_ge(sSSG, V(("fn", j - 4), 0))
                        t.wait_ge(sSVG, V(("bn5", (j - 4) // 2), 0))
                    pf = p_f[j % 4]
                    for h in range(2):
                        ins = t.matmul(pf[:, h * DH:(h + 1) * DH],
                                       sb_alphaT[:, j * P:(j + 1) * P],
                                       sb_sv2[:, h * DH:(h + 1) * DH],
                                       start=True, stop=True)
                    ins.then_inc(sCS, 1)

            def _gp_count(m):
                return sum(1 for j2 in range(m + 1) if j2 % 4 != 1)

            @blk.vector
            def _(v):
                def vadd(jo):
                    v.wait_ge(sSSG, V(("fn", jo), 0))
                    v.tensor_tensor(sb_x[:, jo * D:(jo + 1) * D], fnb[:, (jo % 4) * D:(jo % 4 + 1) * D],
                                    sb_x[:, jo * D:(jo + 1) * D], op=OP.add).then_inc(sX5o, 1)

                def p5tail(q):
                    v.wait_ge(sSSG, V(("rs5", q), 0))
                    VTS(tmp4[:, 48:48 + 2], SC(SM_MX, 2 * q, 2), -0.25, None, OP.mult)
                    val["p5", q] = VTT(SC(SM_B1, 2 * q, 2), tmp4[:, 48:48 + 2], SC(SM_S1, 2 * q, 2), OP.mult)

                for pr in range(NPAIR):
                    if pr >= 1:
                        p5tail(pr - 1)
                    if pr >= 2 and (pr - 2) % 4 == 0:
                        vadd(2 * (pr - 2) + 1)
                    v.wait_ge(sCS, NT + 2 * pr + 2)
                    for j in (2 * pr, 2 * pr + 1):
                        VBN(bn6x[:, j * 12:j * 12 + 6], p_f[j % 4][:, 0:DH])
                        val["bn5", pr] = VBN(bn6x[:, j * 12 + 6:j * 12 + 12], p_f[j % 4][:, DH:D])
                    val["c5", pr] = combine(SM_MX, SM_VX, bn6x, 2 * pr, 2, None)
                p5tail(NPAIR - 1)

            @blk.scalar
            def _(sc):
                if _DBG3:
                    sc.wait_ge(sDBG, 128)
                for j in range(NT + 2):
                    if j % 2 == 0 and j >= 2 and (j - 2) // 2 < NPAIR:
                        pr = (j - 2) // 2
                        sc.wait_ge(sSVG, V(("c5", pr), 0))
                        SACT(SC(SM_G1, 2 * pr, 2), SC(SM_VX, 2 * pr, 2), AF.Ln)
                        val["rs5", pr] = SACT(SC(SM_S1, 2 * pr, 2), SC(SM_G1, 2 * pr, 2), AF.Exp, scale=-0.5)
                    if j >= 2 and (j - 2) < NT:
                        jj = j - 2
                        sc.wait_ge(sSVG, V(("p5", jj // 2), 0))
                        if jj >= 4:
                            tgt = jj - 4
                            if tgt % 8 == 1:
                                sc.wait_ge(sX5o, (tgt - 1) // 8 + 1)
                            else:
                                sc.wait_ge(sX5, sum(1 for t2 in range(tgt + 1) if t2 % 8 != 1))
                        val["fn", jj] = SACT(fnb[:, (jj % 4) * D:(jj % 4 + 1) * D],
                                             p_f[jj % 4][:], AF.Identity,
                                             bias=SC(SM_B1, jj), scale=SC(SM_S1, jj))

            @blk.gpsimd
            def _(g):
                for j in range(NT):
                    if j % 8 == 1:
                        continue
                    g.wait_ge(sSSG, V(("fn", j), 0))
                    g.tensor_tensor(sb_x[:, j * D:(j + 1) * D], fnb[:, (j % 4) * D:(j % 4 + 1) * D],
                                    sb_x[:, j * D:(j + 1) * D], op=OP.add).then_inc(sX5, 1)

            @blk.sync
            def _(s):
                for gg in range(8):
                    m_ = 4 * gg + 3
                    s.wait_ge(sX5, sum(1 for t2 in range(m_ + 1) if t2 % 8 != 1))
                    vc_ = sum(1 for t2 in range(m_ + 1) if t2 % 8 == 1)
                    if vc_:
                        s.wait_ge(sX5o, vc_)
                    s.dma_start(out=out_r[:, 4 * gg:4 * gg + 4, :],
                                in_=sb_x[:].rearrange("p (j c) -> p j c", c=D)[:, 4 * gg:4 * gg + 4, :]
                                ).then_inc(sOUT, 16)
                s.wait_ge(sOUT, 8 * 16)

            es5p.close()
            es4s.close()

        if stage == 2:
            @blk.vector
            def _(v):
                v.wait_ge(sP3T, NT)
                VCP(dbg_sb[0:NS, 0:D], p_si[0:NS, :])
                VCP(dbg_sb[0:NS, D:D + 1], p_d2[0:NS, :])
                VRED(dn_f[:, 0:1], sb_alphaT[:], OP.add)
                val["dn"] = VCP(dbg_sb[0:NS, D + 1:D + 2], dn_f[:, 0:1])

            @blk.gpsimd
            def _(g):
                for gg in range(8):
                    g.wait_ge(sX2, 4 * gg + 4)
                g.dma_start(out=out_r[:, :, :],
                            in_=sb_x[:].rearrange("p (j c) -> p j c", c=D)).then_inc(sOUT, 16)
                g.wait_ge(sSVG, V(("alf", NT - 1), 0))
                g.dma_start(out=dbg_al.rearrange("(j p) c -> p j c", p=P),
                            in_=sb_alpha[:].rearrange("p (j c) -> p j c", c=NS)).then_inc(sDBG, 16)
                g.wait_ge(sSVG, V("dn", 0))
                g.dma_start(out=dbg_si, in_=dbg_sb[0:NS, :]).then_inc(sDBG, 16)
                g.wait_ge(sOUT, 16)
                g.wait_ge(sDBG, 32)

        if stage == 2:
            es3p.close()
            es2b.close()
        es2.close()

    return nc, val


def _consts():
    c_tri = np.triu(np.ones((P, P))).astype(ml_dtypes.bfloat16)
    c_eye = np.eye(P).astype(ml_dtypes.bfloat16)
    t = np.arange(NT * P, dtype=np.float64).reshape(NT, P).T
    c_e2 = (EPS * (t + 1.0) ** 2).astype(np.float32)
    c_one = np.ones((P, P)).astype(ml_dtypes.bfloat16)
    return c_tri, c_eye, c_e2, c_one


def kernel(**inputs):
    _, vals = _build(_STAGE)
    nc, _ = _build(_STAGE, vals)
    c_tri, c_eye, c_e2, c_one = _consts()
    x = np.ascontiguousarray(np.asarray(inputs["L0_attn_x"], dtype=np.float32))
    wdict = {}
    for k, v in inputs.items():
        if k == "L0_attn_x":
            continue
        wdict[k.replace("L0_ffn_", "")] = np.ascontiguousarray(np.asarray(v, dtype=np.float32))
    in_maps = []
    for c in range(NCORES):
        m = {"x": x[c], "c_tri": c_tri, "c_eye": c_eye, "c_e2": c_e2, "c_one": c_one}
        m.update(wdict)
        in_maps.append(m)
    res = run_bass_kernel_spmd(nc, in_maps, core_ids=list(range(NCORES)))
    return np.stack([r["out"] for r in res.results], axis=0)
